# version 14
# baseline (speedup 1.0000x reference)
import contextlib
import math

import numpy as np

import concourse.bass as bass
import concourse.mybir as mybir
from concourse.bass_utils import run_bass_kernel_spmd

F32 = mybir.dt.float32
BF16 = mybir.dt.bfloat16
AF = mybir.ActivationFunctionType
ALU = mybir.AluOpType
AXX = mybir.AxisListType.X

B, S, D = 16, 2048, 1024
NCORES = 8
T = 2 * S
NH, HD = 16, 64
DQ = DKV = 512
SCALE_C = math.sqrt(16 + 512 + 512)
NE, FE = 16, 512
FS = 1024
LN2_10K = -2.0 * math.log(10000.0)
PI = float(np.pi)

BUILD_STAGE = 5
DEBUG_DUMPS = ()


class SemCtr:
    def __init__(self, sem):
        self.sem = sem
        self.n = 0

    def inc(self, ins, k=1):
        ins.then_inc(self.sem, k)
        self.n += k
        return self.n


def build_nc():
    nc = bass.Bass("TRN2", target_bir_lowering=False)

    def din(name, shape):
        return nc.dram_tensor(name, shape, F32, kind="ExternalInput").ap()

    x_in = din("x", [T, D])
    w_dims = {
        "wdq": (D, DQ), "wdkv": (D, DKV), "wuq": (DQ, D), "wqr": (DQ, D),
        "wkr": (D, D), "wuk": (DKV, D), "wuv": (DKV, D), "wo": (D, D),
        "w1t": (NE * D, FE), "w3t": (NE * D, FE), "w2t": (NE * FE, D),
        "w1sts": (2 * D, FE), "w3sts": (2 * D, FE), "w2st": (FS, D),
        "wgt": (D, NE),
    }
    w_in = {k: din(k, [128, (r // 128) * c]) for k, (r, c) in w_dims.items()}
    b_in = {k: din(k, [128, 4]) for k in (
        "bdq", "bdkv", "buqA", "buqB", "bqrA", "bqrB", "bkrA", "bkrB",
        "bukA", "bukB")}
    b_in["buv"] = din("buv", [D])
    b_in["wob"] = din("wob", [D])
    eye_in = din("eye", [128, 128])
    sel_in = din("selc", [128, 512])
    pos_in = din("pos", [128, S])

    y_out = nc.dram_tensor("y", [T, D], F32, kind="ExternalOutput").ap()

    def scr(name, shape):
        return nc.dram_tensor(name, shape, BF16, kind="Internal").ap()

    wscr = {k: scr("s_" + k, [128, (r // 128) * c])
            for k, (r, c) in w_dims.items()}
    wcols = {k: c for k, (r, c) in w_dims.items()}
    xnspill = scr("xnspill", [D, T])
    attnspill = scr("attnspill", [T, D])

    dbg_outs = {}

    _uid = [0]

    def sbt(name, shape, dtype):
        _uid[0] += 1
        return nc.sbuf_tensor(f"{name}_{_uid[0]}", shape, dtype)

    with contextlib.ExitStack() as top:
        ent = top.enter_context

        identf = ent(sbt("identf", [128, 128], F32))
        identb = ent(sbt("identb", [128, 128], BF16))
        onesf = ent(sbt("onesf", [128, 128], F32))
        onesb = ent(sbt("onesb", [128, 128], BF16))
        bias_sb = {k: ent(sbt("b_" + k, [128, 4], F32)) for k in (
            "bdq", "bdkv", "buqA", "buqB", "bqrA", "bqrB", "bkrA", "bkrB",
            "bukA", "bukB")}
        buv_bc = ent(sbt("buv_bc", [128, D], F32))
        wob_row = ent(sbt("wob_row", [1, D], BF16))
        wgate_sb = ent(sbt("wgate_sb", [128, 8 * NE], BF16))
        selb = ent(sbt("selb", [128, 512], BF16))
        self_f = ent(sbt("self_f", [128, 512], F32))
        scores_tok = ent(sbt("scores_tok", [128, 32 * NE], F32))
        combine_tok = ent(sbt("combine_tok", [128, 32 * NE], F32))
        expM = ent(sbt("expM", [128, 1], F32))
        spacer = ent(sbt("spacer", [1, 8], F32))
        theta = ent(sbt("theta", [128, 4], F32))
        iota_p = ent(sbt("iota_p", [128, 1], F32))
        pos_row = ent(sbt("pos_row", [1, 128], F32))
        minf_sb = ent(sbt("minf_sb", [128, NE], F32))

        psum = ent(nc.psum_tensor("psum", [128, 8 * 512], F32))

        def bank(i):
            return psum[:, 512 * i:512 * (i + 1)]

        sem_names = ["ldw", "ldx", "ldm", "cast", "spw", "spx", "mm", "ev",
                     "sc", "vx", "exp", "pvm", "oev", "tp", "ldf", "ldf2",
                     "out", "mm2", "ev2", "spc", "cldA", "cldB", "cldC",
                     "ldxA", "ldxB", "ldfA", "ldfB", "ldf2A", "ldf2B",
                     "outA", "outB", "spwA", "spwB", "spwC", "spwD", "spwE",
                     "spwF", "cldD", "cldE", "cldF", "ldp", "castP",
                     "cldP0", "cldP1", "cldP2", "cldP3",
                     "spwP0", "spwP1", "spwP2", "spwP3", "ldm2"]
        SC = {n: SemCtr(ent(nc.semaphore("sem_" + n))) for n in sem_names}
        block = ent(nc.Block())

        def tbar(t, skip_vx=False):
            for n in ("ev", "ev2", "vx", "sc", "oev"):
                if n == "vx" and skip_vx:
                    continue
                if SC[n].n:
                    t.wait_ge(SC[n].sem, SC[n].n)

        def scope_barrier():
            names = ["ldw", "ldx", "ldm", "spw", "spx", "mm", "ev",
                     "sc", "vx", "exp", "pvm", "oev", "tp", "ldf", "ldf2",
                     "out", "mm2", "ev2"]
            sync_extra = ["cast", "castP",
                          "cldP0", "cldP1", "cldP2", "cldP3",
                          "spwP0", "spwP1", "spwP2", "spwP3"]

            def wait_all(e):
                for n in names:
                    if SC[n].n:
                        e.wait_ge(SC[n].sem, SC[n].n)

            @block.sync
            def _(e):
                wait_all(e)
                for n in sync_extra:
                    if SC[n].n:
                        e.wait_ge(SC[n].sem, SC[n].n)

            @block.tensor
            def _(e):
                wait_all(e)

            @block.vector
            def _(e):
                wait_all(e)

            @block.scalar
            def _(e):
                wait_all(e)
                e.wait_ge(SC["spc"].sem, SC["spc"].n)
                for _sp in range(8):
                    ins = e.copy(spacer[0:1, _sp:_sp + 1], onesf[0:1, 0:1])
                SC["spc"].inc(ins)

            @block.gpsimd
            def _(e):
                wait_all(e)

        @block.sync
        def _(s):
            def ld(dst, src):
                s.dma_start(out=dst, in_=src).then_inc(SC["ldm"].sem, 16)
                SC["ldm"].n += 16
            ld(identf[:], eye_in[:, :])
            ld(self_f[:, :], sel_in[:, :])
            ld(pos_row[0:1, :], pos_in[0:1, 0:128])
            for k in ("bdq", "bdkv", "buqA", "buqB", "bqrA", "bqrB", "bkrA",
                      "bkrB", "bukA", "bukB"):
                ld(bias_sb[k][:, :], b_in[k][:, :])
        ldm_all = SC["ldm"].n

        with contextlib.ExitStack() as p0a:
            buv_row = p0a.enter_context(sbt("buv_row", [1, D], F32))
            wob_rowf = p0a.enter_context(
                sbt("wob_rowf", [1, D], F32))

            @block.sync
            def _(s):
                s.dma_start(out=buv_row[0:1, :], in_=b_in["buv"][None, :]
                            ).then_inc(SC["ldm"].sem, 16)
                SC["ldm"].n += 16
                s.dma_start(out=wob_rowf[0:1, :], in_=b_in["wob"][None, :]
                            ).then_inc(SC["ldm"].sem, 16)
                SC["ldm"].n += 16
            ldm_all2 = SC["ldm"].n

            @block.vector
            def _(v):
                v.wait_ge(SC["ldm"].sem, ldm_all2)
                v.memset(onesf[:], 1.0)
                v.memset(onesb[:], 1.0)
                v.memset(minf_sb[:], -1e30)
                v.tensor_copy(identb[:], identf[:])
                v.tensor_copy(selb[:, :], self_f[:, :])
                v.memset(theta[:, 1:4], 0.0)
                v.tensor_copy(wob_row[0:1, :], wob_rowf[0:1, :])
                SC["vx"].inc(v.engine_nop())
            vx0 = SC["vx"].n

            @block.tensor
            def _(t):
                t.wait_ge(SC["vx"].sem, vx0)
                SC["tp"].inc(t.transpose(psum[0:128, 0:1], pos_row[0:1, 0:128],
                                         identf[0:1, 0:1]))
                t.matmul(bank(2), onesf[0:1, 0:128], buv_row[0:1, 0:512],
                         start=True, stop=True, tile_position=(0, 0))
                SC["mm"].inc(
                    t.matmul(bank(3), onesf[0:1, 0:128],
                             buv_row[0:1, 512:1024],
                             start=True, stop=True, tile_position=(0, 0)))

            @block.vector
            def _(v):
                v.wait_ge(SC["tp"].sem, SC["tp"].n)
                v.tensor_copy(iota_p[:, 0:1], psum[0:128, 0:1])
                v.wait_ge(SC["mm"].sem, SC["mm"].n)
                v.tensor_copy(buv_bc[:, 0:512], bank(2))
                v.tensor_copy(buv_bc[:, 512:1024], bank(3))
                SC["vx"].inc(v.engine_nop())
        vx1 = SC["vx"].n

        @block.scalar
        def _(s):
            s.wait_ge(SC["vx"].sem, vx1)
            s.activation(theta[:, 0:1], iota_p[:, 0:1], AF.Exp,
                         scale=LN2_10K)
            s.wait_ge(SC["spc"].sem, SC["spc"].n)
            for _sp in range(7):
                ins = s.copy(spacer[0:1, _sp:_sp + 1], onesf[0:1, 0:1])
            SC["spc"].inc(ins)
            SC["sc"].inc(s.copy(spacer[0:1, 7:8], onesf[0:1, 0:1]))
        sc0 = SC["sc"].n

        @block.vector
        def _(v):
            v.wait_ge(SC["sc"].sem, sc0)
            SC["vx"].inc(v.engine_nop())
        theta_ready = SC["vx"].n

        cast_list = ["wdq", "wdkv", "wuq", "wqr", "wkr", "wuk", "wuv", "wo",
                     "wgt"]
        castP_list = ["w1sts", "w3sts", "w2st", "w1t", "w3t", "w2t"]
        castA_set = set(cast_list)
        cast_done = {}
        with contextlib.ExitStack() as p0:
            scope_barrier()
            CW = 2048
            cf = [p0.enter_context(sbt(f"c_f{i}", [128, CW], F32))
                  for i in range(6)]
            cb = [p0.enter_context(sbt(f"c_b{i}", [128, CW], BF16))
                  for i in range(6)]
            tiles = []
            for wname in cast_list:
                r_, c_ = w_dims[wname]
                total = (r_ // 128) * c_
                for c0 in range(0, total, CW):
                    tiles.append((wname, c0, min(CW, total - c0)))

            cslot = ("cldA", "cldB", "cldC", "cldD", "cldE", "cldF")
            wslot_ = ("spwA", "spwB", "spwC", "spwD", "spwE", "spwF")

            @block.sync
            def _(s):
                for i, (wname, c0, w) in enumerate(tiles):
                    if i >= 6:
                        s.wait_ge(SC["cast"].sem, i - 5)
                    s.dma_start(out=cf[i % 6][:, 0:w],
                                in_=w_in[wname][0:128, c0:c0 + w]
                                ).then_inc(SC[cslot[i % 6]].sem, 16)
                    SC[cslot[i % 6]].n += 16

            @block.scalar
            def _(sc):
                for i, (wname, c0, w) in enumerate(tiles):
                    sc.wait_ge(SC[cslot[i % 6]].sem, 16 * (i // 6 + 1))
                    if i >= 6:
                        sc.wait_ge(SC[wslot_[i % 6]].sem, 16 * (i // 6))
                    SC["cast"].inc(sc.activation(cb[i % 6][:, 0:w],
                                                 cf[i % 6][:, 0:w],
                                                 AF.Identity))

            @block.gpsimd
            def _(g):
                for i, (wname, c0, w) in enumerate(tiles):
                    g.wait_ge(SC["cast"].sem, i + 1)
                    g.dma_start(out=wscr[wname][0:128, c0:c0 + w],
                                in_=cb[i % 6][:, 0:w]
                                ).then_inc(SC[wslot_[i % 6]].sem, 16)
                    SC[wslot_[i % 6]].n += 16
        spw_totals = tuple((nm_, SC[nm_].n) for nm_ in (
            "spwA", "spwB", "spwC", "spwD", "spwE", "spwF"))

        pump_bufs = {"cf": None, "cb": None}
        pump_tiles_list = []
        for wname in castP_list:
            r_, c_ = w_dims[wname]
            total_ = (r_ // 128) * c_
            for c0_ in range(0, total_, 1024):
                pump_tiles_list.append((wname, c0_, min(1024, total_ - c0_)))
        pump_state = {"i": 0}
        cldP_ = ("cldP0", "cldP1", "cldP2", "cldP3")
        spwP_ = ("spwP0", "spwP1", "spwP2", "spwP3")

        def pump_tiles(n):
            i0 = pump_state["i"]
            i1 = min(i0 + n, len(pump_tiles_list))
            pump_state["i"] = i1
            if i1 == i0:
                return

            cfP = pump_bufs["cf"]
            cbP = pump_bufs["cb"]

            @block.sync
            def _(s):
                for i in range(i0, i1):
                    wname, c0, w = pump_tiles_list[i]
                    if i >= 4:
                        s.wait_ge(SC[spwP_[i % 4]].sem, 16 * (i // 4))
                    s.dma_start(out=cfP[i % 4][:, 0:w],
                                in_=w_in[wname][0:128, c0:c0 + w]
                                ).then_inc(SC[cldP_[i % 4]].sem, 16)
                    SC[cldP_[i % 4]].n += 16

            @block.gpsimd
            def _(g):
                for i in range(i0, i1):
                    wname, c0, w = pump_tiles_list[i]
                    g.wait_ge(SC[cldP_[i % 4]].sem, 16 * (i // 4 + 1))
                    if i >= 4:
                        g.wait_ge(SC[spwP_[i % 4]].sem, 16 * (i // 4))
                    SC["castP"].inc(g.tensor_copy(cbP[i % 4][:, 0:w],
                                                  cfP[i % 4][:, 0:w]))
                    g.wait_ge(SC["castP"].sem, SC["castP"].n)
                    g.dma_start(out=wscr[wname][0:128, c0:c0 + w],
                                in_=cbP[i % 4][:, 0:w]
                                ).then_inc(SC[spwP_[i % 4]].sem, 16)
                    SC[spwP_[i % 4]].n += 16

        @block.gpsimd
        def _(g):
            for nm_, cnt_ in spw_totals:
                g.wait_ge(SC[nm_].sem, cnt_)
            g.dma_start(out=wgate_sb[:, :],
                        in_=wscr["wgt"][0:128, :]
                        ).then_inc(SC["ldm2"].sem, 16)
            SC["ldm2"].n += 16
        wgate_ready = SC["ldm2"].n

        def load_weight_rows(s, wname, row0, nrows, dst, ctr, war_wait=True):
            cols = wcols[wname]
            if wname in castA_set:
                for nm_, cnt_ in spw_totals:
                    s.wait_ge(SC[nm_].sem, cnt_)
            else:
                for nm_ in spwP_:
                    if SC[nm_].n:
                        s.wait_ge(SC[nm_].sem, SC[nm_].n)
            if war_wait and SC["mm"].n:
                s.wait_ge(SC["mm"].sem, SC["mm"].n)
            c0 = (row0 // 128) * cols
            nc_ = (nrows // 128) * cols
            s.dma_start(out=dst[:, 0:nc_],
                        in_=wscr[wname][0:128, c0:c0 + nc_]
                        ).then_inc(ctr.sem, 16)
            ctr.n += 16
            return ctr.n

        def load_weight_bf16(s, wname, dst, ctr, war_wait=True):
            rows = w_dims[wname][0]
            return load_weight_rows(s, wname, 0, rows, dst, ctr,
                                    war_wait=war_wait)

        def proj_fm(wname, nm, TW, src, dst_fn, bias_fn, banks, wstate,
                    dst_off=0, skip_vx=False):
            rows = w_dims[wname][0]
            WC = wcols[wname]
            nk = rows // 128
            ngrp = TW // 512
            jobs = [(m, g) for m in range(nm) for g in range(ngrp)]
            if nk * WC <= 4096:
                slot = wstate["i"] % 2
                wstate["i"] += 1
                wbuf = wstate["bufs"][slot]
                war = wstate["marks"][slot]

                @block.sync
                def _(s):
                    if war:
                        s.wait_ge(SC["mm"].sem, war)
                    load_weight_rows(s, wname, 0, rows, wbuf, SC["ldf"]
                                     if slot == 0 else SC["ldfB"],
                                     war_wait=False)
                wsem = SC["ldf"] if slot == 0 else SC["ldfB"]
                wready = wsem.n
                nbk = len(banks)
                mm_base = SC["mm"].n
                ev_base = SC["ev"].n

                @block.tensor
                def _(t):
                    t.wait_ge(wsem.sem, wready)
                    tbar(t, skip_vx=skip_vx)
                    for j, (m, g) in enumerate(jobs):
                        if j >= nbk:
                            t.wait_ge(SC["ev"].sem, ev_base + j - nbk + 1)
                        bk = bank(banks[j % nbk])
                        for k in range(nk):
                            ins = t.matmul(
                                bk,
                                wbuf[:, k * WC + dst_off + m * 128:
                                     k * WC + dst_off + (m + 1) * 128],
                                src[:, k * TW + g * 512:
                                    k * TW + (g + 1) * 512],
                                start=(k == 0), stop=(k == nk - 1))
                        SC["mm"].inc(ins)
                wstate["marks"][slot] = SC["mm"].n

                vx_mark = SC["vx"].n

                @block.scalar
                def _(s):
                    if skip_vx and vx_mark:
                        s.wait_ge(SC["vx"].sem, vx_mark)
                    for j, (m, g) in enumerate(jobs):
                        s.wait_ge(SC["mm"].sem, mm_base + j + 1)
                        bk = bank(banks[j % nbk])
                        kw = {}
                        bt = bias_fn(m)
                        if bt is not None:
                            kw["bias"] = bt
                        SC["ev"].inc(s.activation(
                            dst_fn(m)[:, g * 512:(g + 1) * 512], bk,
                            AF.Identity, **kw))
            else:
                assert len(jobs) <= 8 and nk % 2 == 0
                hk = nk // 2
                mm_base = SC["mm"].n
                wsems = []
                bufs = []
                for wh in range(2):
                    slot = wstate["i"] % 2
                    wstate["i"] += 1
                    wbuf = wstate["bufs"][slot]
                    war = wstate["marks"][slot]
                    sem = SC["ldf"] if slot == 0 else SC["ldfB"]

                    @block.sync
                    def _(s):
                        if war:
                            s.wait_ge(SC["mm"].sem, war)
                        load_weight_rows(s, wname, wh * hk * 128, hk * 128,
                                         wbuf, sem, war_wait=False)
                    wsems.append((sem, sem.n))
                    bufs.append(wbuf)
                    wstate["marks"][slot] = None

                for wh in range(2):
                    sem, wready = wsems[wh]
                    wbuf = bufs[wh]

                    @block.tensor
                    def _(t):
                        t.wait_ge(sem.sem, wready)
                        if wh == 0:
                            tbar(t, skip_vx=skip_vx)
                        for j, (m, g) in enumerate(jobs):
                            for k in range(hk):
                                ins = t.matmul(
                                    bank(j),
                                    wbuf[:, k * WC + dst_off + m * 128:
                                         k * WC + dst_off + (m + 1) * 128],
                                    src[:, (wh * hk + k) * TW + g * 512:
                                        (wh * hk + k) * TW + (g + 1) * 512],
                                    start=(wh == 0 and k == 0),
                                    stop=(wh == 1 and k == hk - 1))
                            SC["mm"].inc(ins)
                for slot in (0, 1):
                    if wstate["marks"][slot] is None:
                        wstate["marks"][slot] = SC["mm"].n

                vx_mark = SC["vx"].n

                @block.scalar
                def _(s):
                    if skip_vx and vx_mark:
                        s.wait_ge(SC["vx"].sem, vx_mark)
                    for j, (m, g) in enumerate(jobs):
                        s.wait_ge(SC["mm"].sem,
                                  mm_base + len(jobs) + j + 1)
                        kw = {}
                        bt = bias_fn(m)
                        if bt is not None:
                            kw["bias"] = bt
                        SC["ev"].inc(s.activation(
                            dst_fn(m)[:, g * 512:(g + 1) * 512], bank(j),
                            AF.Identity, **kw))

        def dump(name, src_ap, shape, dtype=F32):
            if name not in DEBUG_DUMPS:
                return
            d = nc.dram_tensor("dbg_" + name, shape, dtype,
                               kind="ExternalOutput").ap()
            dbg_outs[name] = d

            @block.gpsimd
            def _(g):
                for n in SC:
                    if n != "out" and SC[n].n:
                        g.wait_ge(SC[n].sem, SC[n].n)
                g.dma_start(out=d[:, :], in_=src_ap).then_inc(SC["out"].sem, 16)
                SC["out"].n += 16

        def gen_rope_tables(tbl):
            cos_sb = tbl.enter_context(sbt("cos_sb", [128, 4 * S], BF16))
            sin_sb = tbl.enter_context(sbt("sin_sb", [128, 4 * S], BF16))
            scope_barrier()
            with contextlib.ExitStack() as tg:
                scrfT = tg.enter_context(sbt("scrfT", [128, 2 * S], F32))
                scrgT = tg.enter_context(sbt("scrgT", [128, S], BF16))
                for c in range(4):
                    @block.sync
                    def _(s):
                        if SC["vx"].n:
                            s.wait_ge(SC["vx"].sem, SC["vx"].n)
                        s.dma_start(out=scrfT[:, 0:S], in_=pos_in[:, :]
                                    ).then_inc(SC["ldp"].sem, 16)
                        SC["ldp"].n += 16
                    m0 = SC["ldp"].n

                    @block.scalar
                    def _(s):
                        s.wait_ge(SC["vx"].sem, theta_ready)
                        s.wait_ge(SC["ldp"].sem, m0)
                        SC["sc"].inc(s.activation(
                            scrfT[:, S:2 * S], scrfT[:, 0:S],
                            AF.Identity, scale=theta[:, c:c + 1]))
                    m1 = SC["sc"].n

                    @block.vector
                    def _(v):
                        v.wait_ge(SC["sc"].sem, m1)
                        SC["vx"].inc(v.tensor_scalar(
                            scrfT[:, 0:S], scrfT[:, S:2 * S],
                            512.0 * PI, None, op0=ALU.is_gt))
                        v.wait_ge(SC["vx"].sem, SC["vx"].n)
                        SC["vx"].inc(v.scalar_tensor_tensor(
                            scrfT[:, S:2 * S], scrfT[:, 0:S],
                            -512.0 * PI, scrfT[:, S:2 * S],
                            op0=ALU.mult, op1=ALU.add))
                        v.wait_ge(SC["vx"].sem, SC["vx"].n)
                        SC["vx"].inc(v.tensor_scalar(
                            scrfT[:, 0:S], scrfT[:, 0:S],
                            -2.0, 1.0, op0=ALU.mult, op1=ALU.add))
                    m2 = SC["vx"].n

                    @block.scalar
                    def _(s):
                        s.wait_ge(SC["vx"].sem, m2)
                        SC["sc"].inc(s.activation(
                            scrgT[:, 0:S], scrfT[:, S:2 * S],
                            AF.Sin, scale=1.0 / 512.0))
                    m3 = SC["sc"].n

                    @block.vector
                    def _(v):
                        v.wait_ge(SC["sc"].sem, m3)
                        v.tensor_tensor(sin_sb[:, c * S:(c + 1) * S],
                                        scrgT[:, 0:S], scrfT[:, 0:S],
                                        op=ALU.mult)
                        SC["vx"].inc(v.tensor_scalar(
                            scrfT[:, S:2 * S], scrfT[:, S:2 * S],
                            -1.0, 512.0 * PI / 2,
                            op0=ALU.mult, op1=ALU.add))
                    m4 = SC["vx"].n

                    @block.scalar
                    def _(s):
                        s.wait_ge(SC["vx"].sem, m4)
                        SC["sc"].inc(s.activation(
                            scrgT[:, 0:S], scrfT[:, S:2 * S],
                            AF.Sin, scale=1.0 / 512.0))
                    m5 = SC["sc"].n

                    @block.vector
                    def _(v):
                        v.wait_ge(SC["sc"].sem, m5)
                        SC["vx"].inc(v.tensor_tensor(
                            cos_sb[:, c * S:(c + 1) * S],
                            scrgT[:, 0:S], scrfT[:, 0:S], op=ALU.mult))

            return cos_sb, sin_sb

        for seq in range(2):
            with contextlib.ExitStack() as sqs:
                qEA = sqs.enter_context(sbt("qEA", [128, 4 * S], BF16))
                qOB = sqs.enter_context(sbt("qOB", [128, 4 * S], BF16))
                kEA = sqs.enter_context(sbt("kEA", [128, 4 * S], BF16))
                kOB = sqs.enter_context(sbt("kOB", [128, 4 * S], BF16))
                v_sb = sqs.enter_context(
                    sbt("v_sb", [128, 16 * 1024], BF16))
                qkmax = sqs.enter_context(sbt("qkmax", [128, 40], F32))
                nqkm = [0]
                tblsc = contextlib.ExitStack()
                cos_sb, sin_sb = gen_rope_tables(tblsc)

                for half in range(2):
                    TW = 1024
                    tok0 = seq * S + half * TW
                    with contextlib.ExitStack() as ph:
                        xnT = ph.enter_context(
                            sbt("xnT", [128, 8 * TW], BF16))
                        ct_q = ph.enter_context(
                            sbt("ct_q", [128, 4 * TW], BF16))
                        ct_kv = ph.enter_context(
                            sbt("ct_kv", [128, 4 * TW], BF16))
                        scrf = ph.enter_context(
                            sbt("scrf", [128, TW], F32))
                        scrg = ph.enter_context(
                            sbt("scrg", [128, TW], BF16))

                        scope_barrier()
                        with contextlib.ExitStack() as p1:
                            scope_barrier()
                            xg = [p1.enter_context(
                                sbt(f"xg{i}", [128, 1024], F32))
                                for i in range(4)]
                            stats = p1.enter_context(
                                sbt("stats", [128, 32], F32))
                            xn_tok = scrg
                            sco = scrf[0:16, 0:TW]
                            for grp in range(2):
                                sb0 = 16 * (grp % 2)
                                lsem = SC["ldxA"] if grp % 2 == 0 else SC["ldxB"]
                                sc_before = SC["sc"].n
                                vx_before = SC["vx"].n

                                @block.sync
                                def _(s):
                                    if grp > 0 or half > 0 or seq > 0:
                                        s.wait_ge(SC["sc"].sem, sc_before)
                                        s.wait_ge(SC["vx"].sem, vx_before)
                                    for i in range(4):
                                        tl = grp * 4 + i
                                        trow = tok0 + tl * 128
                                        s.dma_start(
                                            out=xg[i][:],
                                            in_=x_in[trow:trow + 128, :]
                                        ).then_inc(lsem.sem, 16)
                                        lsem.n += 16
                                g_all = lsem.n

                                @block.vector
                                def _(v):
                                    v.wait_ge(lsem.sem, g_all)
                                    for i in range(4):
                                        SC["vx"].inc(v.reduce_sum(
                                            stats[:, sb0 + i:sb0 + i + 1],
                                            xg[i][:], axis=AXX))
                                vx_s = SC["vx"].n

                                @block.scalar
                                def _(s):
                                    s.wait_ge(SC["tp"].sem, SC["tp"].n)
                                    s.wait_ge(lsem.sem, g_all)
                                    for i in range(4):
                                        if i:
                                            s.wait_ge(SC["sc"].sem, SC["sc"].n)
                                        SC["sc"].inc(s.activation(
                                            xn_tok[:], xg[i][:], AF.Square,
                                            accum_out=stats[:, sb0 + 4 + i:sb0 + 5 + i]))
                                sc_s = SC["sc"].n

                                @block.vector
                                def _(v):
                                    v.wait_ge(SC["vx"].sem, vx_s)
                                    v.wait_ge(SC["sc"].sem, sc_s)

                                    def ch(ins):
                                        SC["vx"].inc(ins)
                                        v.wait_ge(SC["vx"].sem, SC["vx"].n)
                                    ch(v.tensor_scalar_mul(
                                        stats[:, sb0 + 0:sb0 + 4],
                                        stats[:, sb0 + 0:sb0 + 4], 1.0 / D))
                                    ch(v.tensor_scalar_mul(
                                        stats[:, sb0 + 4:sb0 + 8],
                                        stats[:, sb0 + 4:sb0 + 8], 1.0 / D))
                                    ch(v.tensor_tensor(
                                        stats[:, sb0 + 8:sb0 + 12],
                                        stats[:, sb0 + 0:sb0 + 4],
                                        stats[:, sb0 + 0:sb0 + 4], op=ALU.mult))
                                    ch(v.tensor_tensor(
                                        stats[:, sb0 + 4:sb0 + 8],
                                        stats[:, sb0 + 4:sb0 + 8],
                                        stats[:, sb0 + 8:sb0 + 12],
                                        op=ALU.subtract))
                                    SC["vx"].inc(v.tensor_scalar_add(
                                        stats[:, sb0 + 4:sb0 + 8],
                                        stats[:, sb0 + 4:sb0 + 8], 1e-5))
                                vx_v = SC["vx"].n

                                @block.scalar
                                def _(s):
                                    s.wait_ge(SC["vx"].sem, vx_v)
                                    SC["sc"].inc(s.activation(
                                        stats[:, sb0 + 8:sb0 + 12], stats[:, sb0 + 4:sb0 + 8],
                                        AF.Sqrt))
                                sc_v = SC["sc"].n

                                @block.vector
                                def _(v):
                                    v.wait_ge(SC["sc"].sem, sc_v)
                                    SC["vx"].inc(v.reciprocal(
                                        stats[:, sb0 + 8:sb0 + 12],
                                        stats[:, sb0 + 8:sb0 + 12]))
                                vx_r = SC["vx"].n

                                for i in range(4):
                                    tl = grp * 4 + i

                                    @block.vector
                                    def _(v):
                                        v.wait_ge(SC["vx"].sem, vx_r)
                                        v.wait_ge(SC["tp"].sem, SC["tp"].n)
                                        SC["vx"].inc(v.tensor_tensor(
                                            xg[i][:], xg[i][:],
                                            stats[:, sb0 + i:sb0 + i + 1]
                                            .to_broadcast([128, 1024]),
                                            op=ALU.subtract))
                                        v.wait_ge(SC["vx"].sem, SC["vx"].n)
                                        SC["vx"].inc(v.tensor_tensor(
                                            xn_tok[:], xg[i][:],
                                            stats[:, sb0 + 8 + i:sb0 + 9 + i]
                                            .to_broadcast([128, 1024]),
                                            op=ALU.mult))
                                    sc_xn = SC["vx"].n

                                    bkb = bank(tl % 2).bitcast(BF16)

                                    @block.tensor
                                    def _(t):
                                        t.wait_ge(SC["vx"].sem, sc_xn)
                                        if SC["ev"].n:
                                            t.wait_ge(SC["ev"].sem,
                                                      SC["ev"].n - 1)
                                        for c in range(8):
                                            SC["tp"].inc(t.transpose(
                                                bkb[:, c * 128:(c + 1) * 128],
                                                xn_tok[:, c * 128:(c + 1) * 128],
                                                identb[:]))
                                    tp_xn = SC["tp"].n

                                    @block.vector
                                    def _(v):
                                        v.wait_ge(SC["tp"].sem, tp_xn)
                                        SC["ev"].inc(v.tensor_copy(
                                            xnT.rearrange(
                                                "p (c w) -> p c w", w=TW)
                                            [:, :, tl * 128:(tl + 1) * 128],
                                            bkb.rearrange(
                                                "p (c w) -> p c w", w=128)))
                            ev_xn = SC["ev"].n

                            @block.gpsimd
                            def _(g):
                                g.wait_ge(SC["ev"].sem, ev_xn)
                                g.dma_start(
                                    out=xnspill[:, tok0:tok0 + TW]
                                    .rearrange("(c p) t -> p c t", p=128),
                                    in_=xnT.rearrange("p (c t) -> p c t", t=TW)
                                    ).then_inc(SC["spx"].sem, 16)
                                SC["spx"].n += 16

                            mm2_base = SC["mm2"].n

                            @block.tensor
                            def _(t):
                                t.wait_ge(SC["ldm2"].sem, wgate_ready)
                                t.wait_ge(SC["ev"].sem, ev_xn)
                                for g in range(TW // 512):
                                    for k in range(8):
                                        ins = t.matmul(
                                            bank(2 + g)[0:16, :],
                                            wgate_sb[:, k * NE:(k + 1) * NE],
                                            xnT[:, k * TW + g * 512:
                                                k * TW + (g + 1) * 512],
                                            start=(k == 0), stop=(k == 7))
                                    SC["mm2"].inc(ins)
                            mm_gs = SC["mm2"].n

                            @block.vector
                            def _(v):
                                v.wait_ge(SC["mm2"].sem, mm_gs)
                                for g in range(TW // 512):
                                    SC["vx"].inc(v.tensor_copy(
                                        sco[0:16, g * 512:(g + 1) * 512],
                                        bank(2 + g)[0:16, :]))
                            vx_gs = SC["vx"].n

                            for tl in range(8):
                                @block.tensor
                                def _(t):
                                    t.wait_ge(SC["vx"].sem, vx_gs)
                                    if SC["ev2"].n:
                                        t.wait_ge(SC["ev2"].sem, SC["ev2"].n)
                                    SC["tp"].inc(t.transpose(
                                        bank(6)[0:128, 0:16],
                                        sco[0:16, tl * 128:(tl + 1) * 128],
                                        identf[0:16, 0:16]))
                                tpn = SC["tp"].n

                                @block.scalar
                                def _(s):
                                    s.wait_ge(SC["tp"].sem, tpn)
                                    gt = (tok0 // 128) + tl
                                    SC["ev2"].inc(s.activation(
                                        scores_tok[:, gt * NE:(gt + 1) * NE],
                                        bank(6)[0:128, 0:16], AF.Sigmoid))

                        if seq == 0 and half == 0:
                            dump("xnT_h0", xnT[:, :], [128, 8 * TW])
                            dump("hook_h0", xnT[:, 0:1], [128, 1])

                        rpA = ph.enter_context(
                            sbt("rpA", [128, 2 * TW], BF16))
                        rpB = ph.enter_context(
                            sbt("rpB", [128, 2 * TW], BF16))
                        wbufA = ph.enter_context(
                            sbt("wbufA", [128, 4096], BF16))
                        wbufB = ph.enter_context(
                            sbt("wbufB", [128, 4096], BF16))
                        wstate = {"bufs": [wbufA, wbufB], "i": 0,
                                  "marks": {0: 0, 1: 0}}
                        scope_barrier()

                        proj_fm("wdq", 4, TW, xnT,
                                lambda m: ct_q[:, m * TW:(m + 1) * TW],
                                lambda m: bias_sb["bdq"][:, m:m + 1],
                                [0, 1, 2], wstate)
                        proj_fm("wdkv", 4, TW, xnT,
                                lambda m: ct_kv[:, m * TW:(m + 1) * TW],
                                lambda m: bias_sb["bdkv"][:, m:m + 1],
                                [3, 4, 5], wstate)

                        def rope_side(wC, srcC, wR, srcR, bCA, bCB, bRA, bRB,
                                      dstA, dstB):
                            for AB, dA, bb in ((0, dstA, bCA), (1, dstB, bCB)):
                                proj_fm(wC, 4, TW, srcC,
                                        lambda m, dA=dA: dA[
                                            :, m * S + half * TW:
                                            m * S + half * TW + TW],
                                        lambda m, bb=bb: bias_sb[bb][:, m:m + 1],
                                        [0, 1, 2], wstate, dst_off=AB * 512,
                                        skip_vx=True)
                            for cp in range(2):
                                for AB, dP, bb in ((0, rpA, bRA), (1, rpB, bRB)):
                                    proj_fm(wR, 2, TW, srcR,
                                            lambda m, dP=dP: dP[
                                                :, m * TW:(m + 1) * TW],
                                            lambda m, bb=bb, cp=cp:
                                            bias_sb[bb][:, 2 * cp + m:
                                                        2 * cp + m + 1],
                                            [3, 4, 5] if wR != "wkr" else None,
                                            wstate,
                                            dst_off=AB * 512 + cp * 256,
                                            skip_vx=True)
                                ev_pre = SC["ev"].n

                                @block.vector
                                def _(v):
                                    v.wait_ge(SC["ev"].sem, ev_pre)
                                    v.wait_ge(SC["vx"].sem, SC["vx"].n)
                                    v.wait_ge(SC["tp"].sem, SC["tp"].n)

                                    def ch(ins):
                                        SC["vx"].inc(ins)
                                        v.wait_ge(SC["vx"].sem, SC["vx"].n)
                                    for cc in range(2):
                                        c = 2 * cp + cc
                                        aa = rpA[:, cc * TW:(cc + 1) * TW]
                                        bbx = rpB[:, cc * TW:(cc + 1) * TW]
                                        co = cos_sb[:, c * S + half * TW:
                                                    c * S + half * TW + TW]
                                        si = sin_sb[:, c * S + half * TW:
                                                    c * S + half * TW + TW]
                                        dAx = dstA[:, c * S + half * TW:
                                                   c * S + half * TW + TW]
                                        dBx = dstB[:, c * S + half * TW:
                                                   c * S + half * TW + TW]
                                        s1 = scrg[:, 0:TW]
                                        s2 = scrf[:, 0:TW]
                                        ch(v.tensor_tensor(s1, aa, co,
                                                           op=ALU.mult))
                                        ch(v.tensor_tensor(s2, bbx, si,
                                                           op=ALU.mult))
                                        ch(v.tensor_tensor(s1, s1, s2,
                                                           op=ALU.subtract))
                                        ch(v.tensor_tensor(dAx, s1, dAx,
                                                           op=ALU.add))
                                        ch(v.tensor_tensor(s1, bbx, co,
                                                           op=ALU.mult))
                                        ch(v.tensor_tensor(s2, aa, si,
                                                           op=ALU.mult))
                                        ch(v.tensor_tensor(s1, s1, s2,
                                                           op=ALU.add))
                                        ch(v.tensor_tensor(dBx, s1, dBx,
                                                           op=ALU.add))
                            ev_pre2 = SC["ev"].n

                            @block.vector
                            def _(v):
                                v.wait_ge(SC["ev"].sem, ev_pre2)
                                v.wait_ge(SC["vx"].sem, SC["vx"].n)
                                for c in range(4):
                                    v.reduce_max(
                                        qkmax[:, nqkm[0] + c:nqkm[0] + c + 1],
                                        dstA[:, c * S + half * TW:
                                             c * S + half * TW + TW],
                                        axis=AXX, apply_absolute_value=True)
                                for c in range(4):
                                    v.reduce_max(
                                        qkmax[:, nqkm[0] + 4 + c:
                                              nqkm[0] + 5 + c],
                                        dstB[:, c * S + half * TW:
                                             c * S + half * TW + TW],
                                        axis=AXX, apply_absolute_value=True)
                                SC["vx"].inc(v.engine_nop())
                            nqkm[0] += 8

                        rope_side("wuk", ct_kv, "wkr", xnT, "bukA", "bukB",
                                  "bkrA", "bkrB", kEA, kOB)
                        rope_side("wuq", ct_q, "wqr", ct_q, "buqA", "buqB",
                                  "bqrA", "bqrB", qEA, qOB)

                        vslot = wstate["i"] % 2
                        wstate["i"] += 1
                        vbuf = wstate["bufs"][vslot]
                        vwar = wstate["marks"][vslot]
                        vsem = SC["ldf"] if vslot == 0 else SC["ldfB"]

                        @block.sync
                        def _(s):
                            if vwar:
                                s.wait_ge(SC["mm"].sem, vwar)
                            load_weight_bf16(s, "wuv", vbuf, vsem,
                                             war_wait=False)
                        wready = vsem.n
                        mm_base = SC["mm"].n
                        ev_base = SC["ev"].n

                        @block.tensor
                        def _(t):
                            t.wait_ge(vsem.sem, wready)
                            tbar(t, skip_vx=True)
                            for tl in range(8):
                                if tl >= 3:
                                    t.wait_ge(SC["ev"].sem,
                                              ev_base + 2 * (tl - 2))
                                b0 = bank(2 * (tl % 3))
                                b1 = bank(2 * (tl % 3) + 1)
                                for k in range(4):
                                    t.matmul(b0,
                                             ct_kv[:, k * TW + tl * 128:
                                                   k * TW + (tl + 1) * 128],
                                             vbuf[:, k * 1024:k * 1024 + 512],
                                             start=(k == 0), stop=(k == 3))
                                    ins = t.matmul(
                                        b1,
                                        ct_kv[:, k * TW + tl * 128:
                                              k * TW + (tl + 1) * 128],
                                        vbuf[:, k * 1024 + 512:
                                              (k + 1) * 1024],
                                        start=(k == 0), stop=(k == 3))
                                SC["mm"].inc(ins)
                        wstate["marks"][vslot] = SC["mm"].n

                        @block.vector
                        def _(v):
                            for tl in range(8):
                                v.wait_ge(SC["mm"].sem, mm_base + tl + 1)
                                vt = half * 8 + tl
                                b0 = bank(2 * (tl % 3))
                                b1 = bank(2 * (tl % 3) + 1)
                                v.scalar_tensor_tensor(
                                    v_sb[:, vt * 1024:vt * 1024 + 512],
                                    b0, 0.0, buv_bc[:, 0:512],
                                    op0=ALU.bypass, op1=ALU.add)
                                SC["ev"].inc(v.scalar_tensor_tensor(
                                    v_sb[:, vt * 1024 + 512:(vt + 1) * 1024],
                                    b1, 0.0, buv_bc[:, 512:1024],
                                    op0=ALU.bypass, op1=ALU.add), 2)

                    if seq == 0 and half == 0:
                        dump("ctq_h0", ct_q[:, :], [128, 4 * TW])
                        dump("ctkv_h0", ct_kv[:, :], [128, 4 * TW])
                        dump("cos_h0", cos_sb[:, :], [128, 4 * TW], dtype=BF16)
                        dump("sin_h0", sin_sb[:, :], [128, 4 * TW], dtype=BF16)
                        dump("rpA_h0", rpA[:, :], [128, 4 * TW], dtype=BF16)
                        dump("rpB_h0", rpB[:, :], [128, 4 * TW], dtype=BF16)

                if seq == 0:
                    dump("qEA_s0", qEA[:, :], [128, 4 * S])
                    dump("qOB_s0", qOB[:, :], [128, 4 * S])
                    dump("kEA_s0", kEA[:, :], [128, 4 * S])
                    dump("v_s0", v_sb[:, :], [128, 16 * 1024])

                if BUILD_STAGE < 2:
                    continue

                @block.vector
                def _(v):
                    v.wait_ge(SC["vx"].sem, SC["vx"].n)
                    SC["vx"].inc(v.reduce_max(qkmax[:, 32:33],
                                              qkmax[:, 16:32], axis=AXX))
                    v.wait_ge(SC["vx"].sem, SC["vx"].n)
                    SC["vx"].inc(v.reduce_max(qkmax[:, 33:34],
                                              qkmax[:, 0:16], axis=AXX))
                vx_m = SC["vx"].n

                @block.tensor
                def _(t):
                    t.wait_ge(SC["vx"].sem, vx_m)
                    tbar(t)
                    t.transpose(bank(0)[0:1, 0:128], qkmax[:, 32:33],
                                identf[:])
                    SC["tp"].inc(t.transpose(bank(1)[0:1, 0:128],
                                             qkmax[:, 33:34], identf[:]))
                tp_m = SC["tp"].n

                @block.vector
                def _(v):
                    v.wait_ge(SC["tp"].sem, tp_m)
                    SC["vx"].inc(v.reduce_max(qkmax[0:1, 34:35],
                                              bank(0)[0:1, 0:128], axis=AXX))
                    v.wait_ge(SC["vx"].sem, SC["vx"].n)
                    SC["vx"].inc(v.reduce_max(qkmax[0:1, 35:36],
                                              bank(1)[0:1, 0:128], axis=AXX))
                    v.wait_ge(SC["vx"].sem, SC["vx"].n)
                    SC["vx"].inc(v.tensor_tensor(
                        qkmax[0:1, 36:37], qkmax[0:1, 34:35],
                        qkmax[0:1, 35:36], op=ALU.mult))
                vx_m2 = SC["vx"].n

                @block.tensor
                def _(t):
                    t.wait_ge(SC["vx"].sem, vx_m2)
                    SC["mm"].inc(t.matmul(
                        bank(2)[:, 0:1], onesf[0:1, 0:128],
                        qkmax[0:1, 36:37], start=True, stop=True,
                        tile_position=(0, 0)))
                mm_m = SC["mm"].n

                @block.scalar
                def _(s):
                    s.wait_ge(SC["mm"].sem, mm_m)
                    s.activation(expM[:, 0:1], bank(2)[:, 0:1],
                                 AF.Identity, scale=-64.0 / SCALE_C)
                    s.wait_ge(SC["spc"].sem, SC["spc"].n)
                    for _sp in range(7):
                        ins = s.copy(spacer[0:1, _sp:_sp + 1], onesf[0:1, 0:1])
                    SC["spc"].inc(ins)
                    SC["sc"].inc(s.copy(spacer[0:1, 7:8], onesf[0:1, 0:1]))
                sc_m = SC["sc"].n

                if BUILD_STAGE < 3:
                    continue

                tblsc.close()
                QB = 256
                with contextlib.ExitStack() as ae:
                    scope_barrier()
                    pT = ae.enter_context(
                        sbt("pT", [128, 4 * 16 * QB], BF16))
                    attnT = ae.enter_context(
                        sbt("attnT", [128, 8 * QB], BF16))
                    attnF = ae.enter_context(
                        sbt("attnF", [128, 8 * QB], BF16))
                    zrow = ae.enter_context(
                        sbt("zrow", [128, QB], F32))

                    @block.vector
                    def _(v):
                        SC["vx"].inc(v.memset(zrow[:, :], 1.0))
                    rzbc = ae.enter_context(
                        sbt("rzbc", [128, 2 * QB], F32))
                    atok = ae.enter_context(
                        sbt("atok", [128, 2 * 1024], BF16))
                    wo_sb = ae.enter_context(
                        sbt("wo_sb", [128, 8 * 1024], BF16))
                    pump_bufs["cf"] = [ae.enter_context(
                        sbt(f"cP_f{i}", [128, 1024], F32)) for i in range(4)]
                    pump_bufs["cb"] = [ae.enter_context(
                        sbt(f"cP_b{i}", [128, 1024], BF16)) for i in range(4)]

                    @block.sync
                    def _(s):
                        load_weight_bf16(s, "wo", wo_sb, SC["ldf2"])
                    wo_ready = SC["ldf2"].n

                    for qb in range(S // QB):
                        q0 = qb * QB
                        for quad in range(4):
                            exp_base = SC["exp"].n
                            mm_base = SC["mm"].n

                            @block.tensor
                            def _(t):
                                t.wait_ge(SC["sc"].sem, sc_m)
                                tbar(t)
                                for kc in range(16):
                                    p = kc // 2
                                    if kc % 2 == 0 and p >= 2:
                                        t.wait_ge(SC["exp"].sem,
                                                  exp_base
                                                  + 4 * ((p - 2) // 2 + 1))
                                    for j in range(4):
                                        bk = bank(2 * j + p % 2)[
                                            :, (kc % 2) * QB:
                                            (kc % 2) * QB + QB]
                                        t.matmul(
                                            bk,
                                            kEA[32 * j:32 * (j + 1),
                                                quad * S + kc * 128:
                                                quad * S + (kc + 1) * 128],
                                            qEA[32 * j:32 * (j + 1),
                                                quad * S + q0:
                                                quad * S + q0 + QB],
                                            start=(kc % 2 == 0), stop=False,
                                            tile_position=(32 * j, 0))
                                        ins = t.matmul(
                                            bk,
                                            kOB[32 * j:32 * (j + 1),
                                                quad * S + kc * 128:
                                                quad * S + (kc + 1) * 128],
                                            qOB[32 * j:32 * (j + 1),
                                                quad * S + q0:
                                                quad * S + q0 + QB],
                                            start=False, stop=(kc % 2 == 1),
                                            tile_position=(32 * j, 0))
                                        SC["mm"].inc(ins)

                            @block.scalar
                            def _(s):
                                s.wait_ge(SC["sc"].sem, sc_m)
                                for kcg in range(4):
                                    for j in range(4):
                                        s.wait_ge(SC["mm"].sem,
                                                  mm_base
                                                  + 4 * (4 * kcg + 3) + j + 1)
                                        SC["exp"].inc(s.activation(
                                            pT[:, (j * 16 + 4 * kcg) * QB:
                                               (j * 16 + 4 * kcg + 4) * QB],
                                            psum[:, (2 * j) * 512:
                                                 (2 * j + 2) * 512],
                                            AF.Exp, bias=expM[:, 0:1],
                                            scale=1.0 / SCALE_C))
                            exp_all = SC["exp"].n

                            @block.tensor
                            def _(t):
                                t.wait_ge(SC["exp"].sem, exp_all)
                                for kc in range(16):
                                    st = (kc == 0)
                                    sp = (kc == 15)
                                    for j in range(4):
                                        h = 4 * quad + j
                                        ins = t.matmul(
                                            bank(j)[64 * (j % 2):
                                                    64 * (j % 2) + 64, 0:QB],
                                            v_sb[:, kc * 1024 + 64 * h:
                                                 kc * 1024 + 64 * h + 64],
                                            pT[:, (j * 16 + kc) * QB:
                                               (j * 16 + kc + 1) * QB],
                                            start=st, stop=sp,
                                            tile_position=(0, 64 * (j % 2)))
                                        if sp:
                                            SC["pvm"].inc(ins)
                                    for j in range(4):
                                        ins = t.matmul(
                                            bank(4)[32 * j:32 * j + 1, 0:QB],
                                            onesb[:, 0:1],
                                            pT[:, (j * 16 + kc) * QB:
                                               (j * 16 + kc + 1) * QB],
                                            start=st, stop=sp,
                                            tile_position=(0, 32 * j))
                                        if sp:
                                            SC["pvm"].inc(ins)
                            pv_all = SC["pvm"].n

                            @block.vector
                            def _(v):
                                v.wait_ge(SC["pvm"].sem, pv_all)
                                for j in range(4):
                                    SC["vx"].inc(v.tensor_copy(
                                        zrow[32 * j:32 * j + 1, :],
                                        bank(4)[32 * j:32 * j + 1, 0:QB]))
                                v.wait_ge(SC["vx"].sem, SC["vx"].n)
                                SC["vx"].inc(v.reciprocal(zrow[:, :],
                                                          zrow[:, :]))
                            vx_z = SC["vx"].n
                            mmb_base = SC["mm2"].n

                            @block.tensor
                            def _(t):
                                t.wait_ge(SC["vx"].sem, vx_z)
                                for j in range(4):
                                    SC["mm2"].inc(t.matmul(
                                        bank(5 + j // 2)[
                                            64 * (j % 2):64 * (j % 2) + 64,
                                            0:QB],
                                        onesf[32 * j:32 * j + 1, 0:64],
                                        zrow[32 * j:32 * j + 1, :],
                                        start=True, stop=True,
                                        tile_position=(32 * j, 64 * (j % 2))))

                            mm_cur = SC["mm"].n

                            @block.vector
                            def _(v):
                                v.wait_ge(SC["mm"].sem, mm_cur)
                                for j in range(4):
                                    v.wait_ge(SC["mm2"].sem, mmb_base + j + 1)
                                    SC["vx"].inc(v.tensor_copy(
                                        rzbc[64 * (j % 2):64 * (j % 2) + 64,
                                             (j // 2) * QB:
                                             (j // 2 + 1) * QB],
                                        bank(5 + j // 2)[
                                            64 * (j % 2):64 * (j % 2) + 64,
                                            0:QB]))
                                    v.wait_ge(SC["vx"].sem, SC["vx"].n)
                                    h = 4 * quad + j
                                    pp = 64 * (j % 2)
                                    SC["oev"].inc(v.tensor_tensor(
                                        attnT[pp:pp + 64,
                                              (h // 2) * QB:
                                              (h // 2 + 1) * QB],
                                        bank(j)[pp:pp + 64, 0:QB],
                                        rzbc[pp:pp + 64,
                                             (j // 2) * QB:
                                             (j // 2 + 1) * QB],
                                        op=ALU.mult))

                        oev_all = SC["oev"].n
                        mm_base = SC["mm"].n
                        ev_base = SC["ev"].n

                        @block.tensor
                        def _(t):
                            t.wait_ge(SC["ldf2"].sem, wo_ready)
                            t.wait_ge(SC["oev"].sem, oev_all)
                            tbar(t)
                            for m in range(8):
                                if m >= 3:
                                    t.wait_ge(SC["ev"].sem, ev_base + m - 2)
                                for k in range(8):
                                    ins = t.matmul(
                                        bank(m % 3)[:, 0:QB],
                                        wo_sb[:, k * 1024 + m * 128:
                                              k * 1024 + (m + 1) * 128],
                                        attnT[:, k * QB:(k + 1) * QB],
                                        start=(k == 0), stop=(k == 7))
                                SC["mm"].inc(ins)

                        tp_prev = SC["tp"].n

                        @block.scalar
                        def _(s):
                            s.wait_ge(SC["tp"].sem, tp_prev)
                            for m in range(8):
                                s.wait_ge(SC["mm"].sem, mm_base + m + 1)
                                SC["ev"].inc(s.activation(
                                    attnF[:, m * QB:(m + 1) * QB],
                                    bank(m % 3)[:, 0:QB], AF.Identity))
                        ev_wo = SC["ev"].n

                        ev2_base = SC["ev2"].n
                        tp_marks = []

                        @block.tensor
                        def _(t):
                            t.wait_ge(SC["ev"].sem, ev_wo)
                            for u in range(2):
                                bkb = bank(3 + u).bitcast(BF16)
                                for m in range(8):
                                    SC["tp"].inc(t.transpose(
                                        bkb[:, m * 128:(m + 1) * 128],
                                        attnF[:, m * QB + u * 128:
                                              m * QB + (u + 1) * 128],
                                        identb[:]))
                                tp_marks.append(SC["tp"].n)

                        spx_prev = SC["spx"].n

                        @block.vector
                        def _(v):
                            v.wait_ge(SC["spx"].sem, spx_prev)
                            for u in range(2):
                                v.wait_ge(SC["tp"].sem, tp_marks[u])
                                SC["ev2"].inc(v.tensor_copy(
                                    atok[:, u * 1024:(u + 1) * 1024],
                                    bank(3 + u).bitcast(BF16)))
                        ev_at = SC["ev2"].n

                        @block.gpsimd
                        def _(g):
                            g.wait_ge(SC["ev2"].sem, ev_at)
                            trow = seq * S + q0
                            g.dma_start(
                                out=attnspill[trow:trow + QB, :]
                                .rearrange("(u p) d -> p u d", p=128),
                                in_=atok.rearrange("p (u d) -> p u d", d=1024)
                                ).then_inc(SC["spx"].sem, 16)
                            SC["spx"].n += 16

                        pump_tiles(14 if seq == 0 else 13)

        pump_tiles(len(pump_tiles_list))
        dump("xnspill", xnspill[:, :], [D, T], dtype=BF16)
        dump("attnspill", attnspill[:, :], [T, D], dtype=BF16)
        dump("scores_tok", scores_tok[:, :], [128, 32 * NE])

        if BUILD_STAGE >= 4:
            with contextlib.ExitStack() as gs:
                scope_barrier()
                g1 = gs.enter_context(sbt("g1", [128, 512], F32))
                g2 = gs.enter_context(sbt("g2", [128, 512], F32))
                g3 = gs.enter_context(sbt("g3", [128, 96], F32))
                t8 = gs.enter_context(sbt("t8", [128, 8], F32))
                gw = gs.enter_context(sbt("gw", [128, NE], F32))

                @block.vector
                def _(v):
                    v.wait_ge(SC["ev2"].sem, SC["ev2"].n)

                    def ch(ins):
                        SC["vx"].inc(ins)
                        v.wait_ge(SC["vx"].sem, SC["vx"].n)
                    s4 = scores_tok.rearrange("p (tg j) -> p tg j", j=4)
                    pairs = [(0, 1), (0, 2), (0, 3), (1, 2), (1, 3), (2, 3)]
                    for pi, (a, b) in enumerate(pairs):
                        ch(v.tensor_tensor(g2[:, 0:128], s4[:, :, a],
                                           s4[:, :, b], op=ALU.add))
                        if pi == 0:
                            ch(v.tensor_copy(g1[:, 0:128], g2[:, 0:128]))
                        else:
                            ch(v.tensor_tensor(g1[:, 0:128], g1[:, 0:128],
                                               g2[:, 0:128], op=ALU.max))
                    gg = g1[:, 0:128].rearrange("p (t g) -> p t g", g=4)
                    for pi, (a, b) in enumerate(pairs):
                        ch(v.tensor_tensor(g2[:, 128:160], gg[:, :, a],
                                           gg[:, :, b], op=ALU.min))
                        if pi == 0:
                            ch(v.tensor_copy(g3[:, 0:32], g2[:, 128:160]))
                        else:
                            ch(v.tensor_tensor(g3[:, 0:32], g3[:, 0:32],
                                               g2[:, 128:160], op=ALU.max))
                    thr_b = g3[:, 0:32].unsqueeze(2).to_broadcast([128, 32, 4])
                    ch(v.tensor_tensor(gg, gg, thr_b, op=ALU.is_ge))
                    alw_b = gg.unsqueeze(3).to_broadcast([128, 32, 4, 4])
                    s_4d = scores_tok.rearrange("p (t g j) -> p t g j",
                                                g=4, j=4)
                    m_4d = g2[:, 0:512].rearrange("p (t g j) -> p t g j",
                                                  g=4, j=4)
                    ch(v.tensor_tensor(m_4d, s_4d, alw_b, op=ALU.mult))
                    ch(v.tensor_scalar(g1[:, 128:256].rearrange(
                        "p (t g) -> p t g", g=4), gg, 1e30, -1e30,
                        op0=ALU.mult, op1=ALU.add))
                    neg_b = g1[:, 128:256].rearrange(
                        "p (t g) -> p t g", g=4).unsqueeze(3).to_broadcast(
                        [128, 32, 4, 4])
                    ch(v.tensor_tensor(m_4d, m_4d, neg_b, op=ALU.add))
                    for tl in range(32):
                        ch(v.max(out=t8[:, :],
                                 in_=g2[:, tl * 16:(tl + 1) * 16]))
                        ch(v.tensor_tensor(g3[:, 64:80],
                                           g2[:, tl * 16:(tl + 1) * 16],
                                           t8[:, 3:4].to_broadcast([128, 16]),
                                           op=ALU.is_ge))
                        ch(v.tensor_tensor(
                            gw[:, :], scores_tok[:, tl * 16:(tl + 1) * 16],
                            g3[:, 64:80], op=ALU.mult))
                        ch(v.reduce_sum(g3[:, 80:81], gw[:, :], axis=AXX))
                        ch(v.tensor_scalar_add(g3[:, 80:81], g3[:, 80:81],
                                               1e-6))
                        ch(v.reciprocal(g3[:, 80:81], g3[:, 80:81]))
                        ch(v.tensor_tensor(
                            combine_tok[:, tl * 16:(tl + 1) * 16], gw[:, :],
                            g3[:, 80:81].to_broadcast([128, 16]),
                            op=ALU.mult))

            dump("combine_tok", combine_tok[:, :], [128, 32 * NE])

        if BUILD_STAGE >= 5:
            with contextlib.ExitStack() as p3:
                scope_barrier()
                h_all = p3.enter_context(
                    sbt("h_all", [128, 72 * 512], BF16))
                xn_tb = [p3.enter_context(
                    sbt(f"xn_tb{i}", [128, 8 * 512], BF16))
                    for i in range(2)]
                at_tb = [p3.enter_context(
                    sbt(f"at_tb{i}", [128, 4 * 1024], BF16))
                    for i in range(2)]
                x_tb = [p3.enter_context(
                    sbt(f"x_tb{i}", [128, 1024], F32))
                    for i in range(4)]
                w13 = [p3.enter_context(
                    sbt(f"w13_{i}", [128, 8 * 512], BF16))
                    for i in range(4)]
                w2b = [p3.enter_context(
                    sbt(f"w2b_{i}", [128, 4 * 1024], BF16))
                    for i in range(2)]
                cbc = p3.enter_context(sbt("cbc", [128, 512], F32))
                sg = p3.enter_context(sbt("sg", [128, 2 * 512], BF16))
                us = p3.enter_context(sbt("us", [128, 2 * 512], BF16))
                ct4 = p3.enter_context(
                    sbt("ct4", [128, 4 * 512], BF16))
                out_sb = [p3.enter_context(
                    sbt(f"out_sb{i}", [128, 1024], F32))
                    for i in range(2)]

                spx_need = SC["spx"].n

                for tb in range(8):
                    tok0 = tb * 512

                    ld_before = SC["mm"].n
                    ev2_before = SC["ev2"].n
                    lsem = SC["ldxA"] if tb % 2 == 0 else SC["ldxB"]

                    @block.sync
                    def _(s):
                        s.wait_ge(SC["spx"].sem, spx_need)
                        if tb >= 1:
                            s.wait_ge(SC["ev2"].sem, ev2_before)
                        if tb >= 2:
                            s.wait_ge(SC["mm"].sem, ld_before)
                        s.dma_start(out=xn_tb[tb % 2]
                                    .rearrange("p (c t) -> p c t", t=512),
                                    in_=xnspill[:, tok0:tok0 + 512]
                                    .rearrange("(c p) t -> p c t", p=128)
                                    ).then_inc(lsem.sem, 16)
                        lsem.n += 16
                        s.dma_start(out=at_tb[tb % 2]
                                    .rearrange("p (u d) -> p u d", d=1024),
                                    in_=attnspill[tok0:tok0 + 512, :]
                                    .rearrange("(u p) d -> p u d", p=128)
                                    ).then_inc(lsem.sem, 16)
                        lsem.n += 16
                        for u in range(4):
                            s.dma_start(out=x_tb[u][:, :],
                                        in_=x_in[tok0 + u * 128:
                                                 tok0 + (u + 1) * 128, :]
                                        ).then_inc(lsem.sem, 16)
                            lsem.n += 16
                    xn_ready = lsem.n
                    xn_sem = lsem

                    @block.tensor
                    def _(t):
                        if BUILD_STAGE >= 4:
                            t.wait_ge(SC["vx"].sem, SC["vx"].n)
                        tbar(t)
                        for u in range(4):
                            gt = tb * 4 + u
                            for b_ in range(4):
                                SC["tp"].inc(t.transpose(
                                    bank(4 + b_)[0:4, u * 128:(u + 1) * 128],
                                    combine_tok[:, gt * 16 + b_:
                                                gt * 16 + 16:4],
                                    identf[:]))
                    tp_c = SC["tp"].n

                    @block.vector
                    def _(v):
                        v.wait_ge(SC["tp"].sem, tp_c)
                        for b_ in range(4):
                            SC["ev2"].inc(v.tensor_copy(
                                ct4[0:4, b_ * 512:(b_ + 1) * 512],
                                bank(4 + b_)[0:4, 0:512]))
                    ct4_ready = SC["ev2"].n

                    pa_mm_marks = {}
                    pa_sc_marks = {}
                    pa_vx_marks = {}
                    for e in range(18):
                        routed = e < NE
                        wslot = e % 2
                        wsem = SC["ldfA"] if wslot == 0 else SC["ldfB"]

                        @block.sync
                        def _(s):
                            if e >= 2:
                                s.wait_ge(SC["mm"].sem, pa_mm_marks[e - 2])
                            w1n = "w1t" if routed else "w1sts"
                            w3n = "w3t" if routed else "w3sts"
                            r0 = e * D if routed else (e - NE) * D
                            load_weight_rows(s, w1n, r0, D,
                                             w13[2 * wslot], wsem,
                                             war_wait=False)
                            load_weight_rows(s, w3n, r0, D,
                                             w13[2 * wslot + 1], wsem,
                                             war_wait=False)
                        w_ready = wsem.n

                        if routed:
                            a_, b_ = e // 4, e % 4
                            cbank = 6 + e % 2

                            @block.tensor
                            def _(t):
                                t.wait_ge(SC["ev2"].sem, ct4_ready)
                                if e == 0:
                                    tbar(t)
                                if e >= 2:
                                    t.wait_ge(SC["vx"].sem,
                                              pa_vx_marks.get(
                                                  ("cbc", e - 2), 0))
                                SC["mm2"].inc(t.matmul(
                                    bank(cbank),
                                    selb[0:4, a_ * 128:(a_ + 1) * 128],
                                    ct4[0:4, b_ * 512:(b_ + 1) * 512],
                                    start=True, stop=True))
                            cbc_mm = SC["mm2"].n

                            @block.vector
                            def _(v):
                                v.wait_ge(SC["mm2"].sem, cbc_mm)
                                SC["vx"].inc(v.tensor_copy(cbc[:, :],
                                                           bank(cbank)))
                            pa_vx_marks[("cbc", e)] = SC["vx"].n
                        cbc_ready = SC["vx"].n

                        mm_base = SC["mm"].n
                        sc_base = SC["sc"].n
                        vx_base = SC["vx"].n

                        @block.tensor
                        def _(t):
                            t.wait_ge(wsem.sem, w_ready)
                            t.wait_ge(xn_sem.sem, xn_ready)
                            if e == 0:
                                tbar(t)
                            for fc in range(4):
                                gidx = e * 4 + fc
                                if gidx >= 3:
                                    gprev = gidx - 3
                                    if gprev >= e * 4:
                                        scm = sc_base + (gprev - e * 4) + 1
                                        vxm = vx_base + 2 * (gprev - e * 4) + 2
                                    else:
                                        scm = pa_sc_marks[gprev]
                                        vxm = pa_vx_marks[gprev]
                                    t.wait_ge(SC["sc"].sem, scm)
                                    t.wait_ge(SC["vx"].sem, vxm)
                                gb = bank(gidx % 3)
                                ub = bank(3 + gidx % 3)
                                for k in range(8):
                                    t.matmul(gb,
                                             w13[2 * wslot][
                                                 :, k * 512 + fc * 128:
                                                 k * 512 + (fc + 1) * 128],
                                             xn_tb[tb % 2][
                                                 :, k * 512:(k + 1) * 512],
                                             start=(k == 0), stop=(k == 7))
                                ins = None
                                for k in range(8):
                                    ins = t.matmul(
                                        ub,
                                        w13[2 * wslot + 1][
                                            :, k * 512 + fc * 128:
                                            k * 512 + (fc + 1) * 128],
                                        xn_tb[tb % 2][:, k * 512:
                                                      (k + 1) * 512],
                                        start=(k == 0), stop=(k == 7))
                                SC["mm"].inc(ins, 2)
                        pa_mm_marks[e] = SC["mm"].n

                        @block.scalar
                        def _(s):
                            for fc in range(4):
                                gidx = e * 4 + fc
                                s.wait_ge(SC["mm"].sem, mm_base + 2 * fc + 2)
                                if gidx >= 2:
                                    gprev = gidx - 2
                                    if gprev >= e * 4:
                                        vxm = vx_base + 2 * (gprev - e * 4) + 2
                                    else:
                                        vxm = pa_vx_marks[gprev]
                                    s.wait_ge(SC["vx"].sem, vxm)
                                SC["sc"].inc(s.activation(
                                    sg[:, (fc % 2) * 512:(fc % 2 + 1) * 512],
                                    bank(gidx % 3), AF.Silu))
                                pa_sc_marks[gidx] = SC["sc"].n

                        @block.vector
                        def _(v):
                            if routed:
                                v.wait_ge(SC["vx"].sem, cbc_ready)
                            for fc in range(4):
                                gidx = e * 4 + fc
                                v.wait_ge(SC["mm"].sem, mm_base + 2 * fc + 2)
                                if routed:
                                    SC["vx"].inc(v.scalar_tensor_tensor(
                                        us[:, (fc % 2) * 512:
                                           (fc % 2 + 1) * 512],
                                        bank(3 + gidx % 3), 0.0, cbc[:, :],
                                        op0=ALU.bypass, op1=ALU.mult))
                                else:
                                    SC["vx"].inc(v.tensor_copy(
                                        us[:, (fc % 2) * 512:
                                           (fc % 2 + 1) * 512],
                                        bank(3 + gidx % 3)))
                                v.wait_ge(SC["vx"].sem, SC["vx"].n)
                                v.wait_ge(SC["sc"].sem, sc_base + fc + 1)
                                SC["vx"].inc(v.tensor_tensor(
                                    h_all[:, (4 * e + fc) * 512:
                                          (4 * e + fc + 1) * 512],
                                    sg[:, (fc % 2) * 512:(fc % 2 + 1) * 512],
                                    us[:, (fc % 2) * 512:(fc % 2 + 1) * 512],
                                    op=ALU.mult))
                                pa_vx_marks[gidx] = SC["vx"].n
                    h_done = SC["vx"].n

                    started = [[False] * 2 for _ in range(4)]

                    def w2_mms(t, wsb, fk, col0):
                        ins = None
                        for u in range(4):
                            for hf in range(2):
                                st = not started[u][hf]
                                started[u][hf] = True
                                ins = t.matmul(
                                    bank(2 * u + hf),
                                    h_all[:, (col0 + fk) * 512 + u * 128:
                                          (col0 + fk) * 512 + (u + 1) * 128],
                                    wsb[:, fk * 1024 + hf * 512:
                                        fk * 1024 + (hf + 1) * 512],
                                    start=st, stop=False)
                        return ins

                    pb_mm_marks = {}
                    for e in range(18):
                        w2sem = SC["ldf2A"] if e % 2 == 0 else SC["ldf2B"]

                        @block.sync
                        def _(s):
                            if e >= 2:
                                s.wait_ge(SC["mm"].sem, pb_mm_marks[e - 2])
                            wn = "w2t" if e < NE else "w2st"
                            r0 = e * FE if e < NE else (e - NE) * FE
                            load_weight_rows(s, wn, r0, FE, w2b[e % 2], w2sem,
                                             war_wait=False)
                        w2_ready = w2sem.n

                        @block.tensor
                        def _(t):
                            t.wait_ge(w2sem.sem, w2_ready)
                            if e == 0:
                                t.wait_ge(SC["vx"].sem, h_done)
                                tbar(t)
                            for fk in range(4):
                                ins = w2_mms(t, w2b[e % 2], fk, 4 * e)
                            SC["mm"].inc(ins)
                        pb_mm_marks[e] = SC["mm"].n

                    mm_y_marks = []

                    @block.tensor
                    def _(t):
                        for u in range(4):
                            for hf in range(2):
                                t.matmul(bank(2 * u + hf),
                                         onesb[0:1, 0:128],
                                         wob_row[0:1, hf * 512:(hf + 1) * 512],
                                         start=False, stop=False,
                                         tile_position=(0, 0))
                                ins = t.matmul(
                                    bank(2 * u + hf), identb[:],
                                    at_tb[tb % 2][:, u * 1024 + hf * 512:
                                                  u * 1024 + (hf + 1) * 512],
                                    start=False, stop=True)
                                SC["mm"].inc(ins)
                            mm_y_marks.append(SC["mm"].n)

                    out_marks = []

                    @block.vector
                    def _(v):
                        v.wait_ge(xn_sem.sem, xn_ready)
                        if SC["outA"].n:
                            v.wait_ge(SC["outA"].sem, SC["outA"].n)
                        if SC["outB"].n:
                            v.wait_ge(SC["outB"].sem, SC["outB"].n)
                        for u in range(4):
                            v.wait_ge(SC["mm"].sem, mm_y_marks[u])
                            if u >= 2:
                                v.wait_ge(SC["outA" if u % 2 == 0 else "outB"].sem,
                                          out_marks[u - 2])
                            for hf in range(2):
                                SC["ev2"].inc(v.tensor_tensor(
                                    out_sb[u % 2][:, hf * 512:(hf + 1) * 512],
                                    bank(2 * u + hf),
                                    x_tb[u][:, hf * 512:(hf + 1) * 512],
                                    op=ALU.add))
                            evn = SC["ev2"].n

                            osem = SC["outA" if u % 2 == 0 else "outB"]

                            @block.gpsimd
                            def _(g):
                                g.wait_ge(SC["ev2"].sem, evn)
                                g.dma_start(
                                    out=y_out[tok0 + u * 128:
                                              tok0 + (u + 1) * 128, :],
                                    in_=out_sb[u % 2][:, :]
                                ).then_inc(osem.sem, 16)
                                osem.n += 16
                            out_marks.append(osem.n)

            outA_all, outB_all = SC["outA"].n, SC["outB"].n

            @block.scalar
            def _(s):
                s.wait_ge(SC["outA"].sem, outA_all)
                s.wait_ge(SC["outB"].sem, outB_all)

        @block.gpsimd
        def _(g):
            g.wait_ge(SC["spx"].sem, SC["spx"].n)
            for nm_ in ("out", "outA", "outB"):
                if SC[nm_].n:
                    g.wait_ge(SC[nm_].sem, SC[nm_].n)

    return nc, dbg_outs


_PERM = np.concatenate([np.arange(0, D, 2), np.arange(1, D, 2)])


def _sel_const():
    sel = np.zeros((128, 512), dtype=np.float32)
    for a in range(4):
        sel[a, a * 128:(a + 1) * 128] = 1.0
    return sel


def make_maps(inputs):
    g = lambda k: np.asarray(inputs[k], dtype=np.float32)
    x = g("L0_attn_x")
    perm = _PERM
    common = {
        "wdq": np.ascontiguousarray(g("L0_attn_Wd_q_w")),
        "wdkv": np.ascontiguousarray(g("L0_attn_Wd_kv_w")),
        "wuq": np.ascontiguousarray(g("L0_attn_Wu_q_w")[:, perm]),
        "wqr": np.ascontiguousarray(g("L0_attn_Wq_r_w")[:, perm]),
        "wkr": np.ascontiguousarray(g("L0_attn_Wk_r_w")[:, perm]),
        "wuk": np.ascontiguousarray(g("L0_attn_Wu_k_w")[:, perm]),
        "wuv": np.ascontiguousarray(g("L0_attn_Wu_v_w")),
        "wo": np.ascontiguousarray(g("L0_attn_Wo_w")),
        "w1t": np.ascontiguousarray(
            g("L0_ffn_w1_e").transpose(0, 2, 1).reshape(NE * D, FE)),
        "w3t": np.ascontiguousarray(
            g("L0_ffn_w3_e").transpose(0, 2, 1).reshape(NE * D, FE)),
        "w2t": np.ascontiguousarray(
            g("L0_ffn_w2_e").transpose(0, 2, 1).reshape(NE * FE, D)),
        "w1sts": np.ascontiguousarray(np.concatenate(
            [g("L0_ffn_w1_s").T[:, 0:512], g("L0_ffn_w1_s").T[:, 512:1024]],
            axis=0)),
        "w3sts": np.ascontiguousarray(np.concatenate(
            [g("L0_ffn_w3_s").T[:, 0:512], g("L0_ffn_w3_s").T[:, 512:1024]],
            axis=0)),
        "w2st": np.ascontiguousarray(g("L0_ffn_w2_s").T),
        "wgt": np.ascontiguousarray(g("L0_ffn_w_gate").T),
        "bdq": np.ascontiguousarray(g("L0_attn_Wd_q_b").reshape(4, 128).T),
        "bdkv": np.ascontiguousarray(g("L0_attn_Wd_kv_b").reshape(4, 128).T),
        "buqA": np.ascontiguousarray(
            g("L0_attn_Wu_q_b")[perm][0:512].reshape(4, 128).T),
        "buqB": np.ascontiguousarray(
            g("L0_attn_Wu_q_b")[perm][512:1024].reshape(4, 128).T),
        "bqrA": np.ascontiguousarray(
            g("L0_attn_Wq_r_b")[perm][0:512].reshape(4, 128).T),
        "bqrB": np.ascontiguousarray(
            g("L0_attn_Wq_r_b")[perm][512:1024].reshape(4, 128).T),
        "bkrA": np.ascontiguousarray(
            g("L0_attn_Wk_r_b")[perm][0:512].reshape(4, 128).T),
        "bkrB": np.ascontiguousarray(
            g("L0_attn_Wk_r_b")[perm][512:1024].reshape(4, 128).T),
        "bukA": np.ascontiguousarray(
            g("L0_attn_Wu_k_b")[perm][0:512].reshape(4, 128).T),
        "bukB": np.ascontiguousarray(
            g("L0_attn_Wu_k_b")[perm][512:1024].reshape(4, 128).T),
        "buv": np.ascontiguousarray(g("L0_attn_Wu_v_b")),
        "wob": np.ascontiguousarray(g("L0_attn_Wo_b")),
        "eye": np.eye(128, dtype=np.float32),
        "selc": _sel_const(),
        "pos": np.ascontiguousarray(
            np.broadcast_to(np.arange(S, dtype=np.float32), (128, S))),
    }
    for k_ in ("wdq", "wdkv", "wuq", "wqr", "wkr", "wuk", "wuv", "wo",
               "w1t", "w3t", "w2t", "w1sts", "w3sts", "w2st", "wgt"):
        w_ = common[k_]
        r_, c_ = w_.shape
        common[k_] = np.ascontiguousarray(
            w_.reshape(r_ // 128, 128, c_).transpose(1, 0, 2)
            .reshape(128, (r_ // 128) * c_))
    maps = []
    for i in range(NCORES):
        m = dict(common)
        m["x"] = np.ascontiguousarray(x[2 * i:2 * i + 2].reshape(T, D))
        maps.append(m)
    return maps


def kernel(**inputs):
    import jax
    jax.device_put(np.zeros(64, np.float32),
                   jax.devices()[0]).block_until_ready()
    nc, _ = build_nc()
    maps = make_maps(inputs)
    res = run_bass_kernel_spmd(nc, maps, core_ids=list(range(NCORES)))
    out = np.empty((B, S, D), dtype=np.float32)
    for i in range(NCORES):
        out[2 * i:2 * i + 2] = res.results[i]["y"].reshape(2, S, D)
    return out


# version 15 (final)
# speedup vs baseline: 1.0051x; 1.0051x over previous
import contextlib
import math

import numpy as np

import concourse.bass as bass
import concourse.mybir as mybir
from concourse.bass_utils import run_bass_kernel_spmd

F32 = mybir.dt.float32
BF16 = mybir.dt.bfloat16
AF = mybir.ActivationFunctionType
ALU = mybir.AluOpType
AXX = mybir.AxisListType.X

B, S, D = 16, 2048, 1024
NCORES = 8
T = 2 * S
NH, HD = 16, 64
DQ = DKV = 512
SCALE_C = math.sqrt(16 + 512 + 512)
NE, FE = 16, 512
FS = 1024
LN2_10K = -2.0 * math.log(10000.0)
PI = float(np.pi)

BUILD_STAGE = 5
DEBUG_DUMPS = ()


class SemCtr:
    def __init__(self, sem):
        self.sem = sem
        self.n = 0

    def inc(self, ins, k=1):
        ins.then_inc(self.sem, k)
        self.n += k
        return self.n


def build_nc():
    nc = bass.Bass("TRN2", target_bir_lowering=False)

    def din(name, shape):
        return nc.dram_tensor(name, shape, F32, kind="ExternalInput").ap()

    x_in = din("x", [T, D])
    w_dims = {
        "wdq": (D, DQ), "wdkv": (D, DKV), "wuq": (DQ, D), "wqr": (DQ, D),
        "wkr": (D, D), "wuk": (DKV, D), "wuv": (DKV, D), "wo": (D, D),
        "w1t": (NE * D, FE), "w3t": (NE * D, FE), "w2t": (NE * FE, D),
        "w1sts": (2 * D, FE), "w3sts": (2 * D, FE), "w2st": (FS, D),
        "wgt": (D, NE),
    }
    w_in = {k: din(k, [128, (r // 128) * c]) for k, (r, c) in w_dims.items()}
    b_in = {k: din(k, [128, 4]) for k in (
        "bdq", "bdkv", "buqA", "buqB", "bqrA", "bqrB", "bkrA", "bkrB",
        "bukA", "bukB")}
    b_in["buv"] = din("buv", [D])
    b_in["wob"] = din("wob", [D])
    eye_in = din("eye", [128, 128])
    sel_in = din("selc", [128, 512])
    pos_in = din("pos", [128, S])

    y_out = nc.dram_tensor("y", [T, D], F32, kind="ExternalOutput").ap()

    def scr(name, shape):
        return nc.dram_tensor(name, shape, BF16, kind="Internal").ap()

    wscr = {k: scr("s_" + k, [128, (r // 128) * c])
            for k, (r, c) in w_dims.items()}
    wcols = {k: c for k, (r, c) in w_dims.items()}
    xnspill = scr("xnspill", [D, T])
    attnspill = scr("attnspill", [T, D])

    dbg_outs = {}

    _uid = [0]

    def sbt(name, shape, dtype):
        _uid[0] += 1
        return nc.sbuf_tensor(f"{name}_{_uid[0]}", shape, dtype)

    with contextlib.ExitStack() as top:
        ent = top.enter_context

        identf = ent(sbt("identf", [128, 128], F32))
        identb = ent(sbt("identb", [128, 128], BF16))
        onesf = ent(sbt("onesf", [128, 128], F32))
        onesb = ent(sbt("onesb", [128, 128], BF16))
        bias_sb = {k: ent(sbt("b_" + k, [128, 4], F32)) for k in (
            "bdq", "bdkv", "buqA", "buqB", "bqrA", "bqrB", "bkrA", "bkrB",
            "bukA", "bukB")}
        buv_bc = ent(sbt("buv_bc", [128, D], F32))
        wob_row = ent(sbt("wob_row", [1, D], BF16))
        wgate_sb = ent(sbt("wgate_sb", [128, 8 * NE], BF16))
        selb = ent(sbt("selb", [128, 512], BF16))
        self_f = ent(sbt("self_f", [128, 512], F32))
        scores_tok = ent(sbt("scores_tok", [128, 32 * NE], F32))
        combine_tok = ent(sbt("combine_tok", [128, 32 * NE], F32))
        expM = ent(sbt("expM", [128, 1], F32))
        spacer = ent(sbt("spacer", [1, 8], F32))
        theta = ent(sbt("theta", [128, 4], F32))
        iota_p = ent(sbt("iota_p", [128, 1], F32))
        pos_row = ent(sbt("pos_row", [1, 128], F32))
        minf_sb = ent(sbt("minf_sb", [128, NE], F32))

        psum = ent(nc.psum_tensor("psum", [128, 8 * 512], F32))

        def bank(i):
            return psum[:, 512 * i:512 * (i + 1)]

        sem_names = ["ldw", "ldx", "ldm", "cast", "spw", "spx", "mm", "ev",
                     "sc", "vx", "exp", "pvm", "oev", "tp", "ldf", "ldf2",
                     "out", "mm2", "ev2", "spc", "cldA", "cldB", "cldC",
                     "ldxA", "ldxB", "ldfA", "ldfB", "ldf2A", "ldf2B",
                     "outA", "outB", "spwA", "spwB", "spwC", "spwD", "spwE",
                     "spwF", "cldD", "cldE", "cldF", "ldp", "castP",
                     "cldP0", "cldP1", "cldP2", "cldP3",
                     "spwP0", "spwP1", "spwP2", "spwP3", "ldm2"]
        SC = {n: SemCtr(ent(nc.semaphore("sem_" + n))) for n in sem_names}
        block = ent(nc.Block())

        def tbar(t, skip_vx=False):
            for n in ("ev", "ev2", "vx", "sc", "oev"):
                if n == "vx" and skip_vx:
                    continue
                if SC[n].n:
                    t.wait_ge(SC[n].sem, SC[n].n)

        def scope_barrier():
            names = ["ldw", "ldx", "ldm", "spw", "spx", "mm", "ev",
                     "sc", "vx", "exp", "pvm", "oev", "tp", "ldf", "ldf2",
                     "out", "mm2", "ev2"]
            sync_extra = ["cast", "castP",
                          "cldP0", "cldP1", "cldP2", "cldP3",
                          "spwP0", "spwP1", "spwP2", "spwP3"]

            def wait_all(e):
                for n in names:
                    if SC[n].n:
                        e.wait_ge(SC[n].sem, SC[n].n)

            @block.sync
            def _(e):
                wait_all(e)
                for n in sync_extra:
                    if SC[n].n:
                        e.wait_ge(SC[n].sem, SC[n].n)

            @block.tensor
            def _(e):
                wait_all(e)

            @block.vector
            def _(e):
                wait_all(e)

            @block.scalar
            def _(e):
                wait_all(e)
                e.wait_ge(SC["spc"].sem, SC["spc"].n)
                for _sp in range(8):
                    ins = e.copy(spacer[0:1, _sp:_sp + 1], onesf[0:1, 0:1])
                SC["spc"].inc(ins)

            @block.gpsimd
            def _(e):
                wait_all(e)

        @block.sync
        def _(s):
            def ld(dst, src):
                s.dma_start(out=dst, in_=src).then_inc(SC["ldm"].sem, 16)
                SC["ldm"].n += 16
            ld(identf[:], eye_in[:, :])
            ld(self_f[:, :], sel_in[:, :])
            ld(pos_row[0:1, :], pos_in[0:1, 0:128])
            for k in ("bdq", "bdkv", "buqA", "buqB", "bqrA", "bqrB", "bkrA",
                      "bkrB", "bukA", "bukB"):
                ld(bias_sb[k][:, :], b_in[k][:, :])
        ldm_all = SC["ldm"].n

        with contextlib.ExitStack() as p0a:
            buv_row = p0a.enter_context(sbt("buv_row", [1, D], F32))
            wob_rowf = p0a.enter_context(
                sbt("wob_rowf", [1, D], F32))

            @block.sync
            def _(s):
                s.dma_start(out=buv_row[0:1, :], in_=b_in["buv"][None, :]
                            ).then_inc(SC["ldm"].sem, 16)
                SC["ldm"].n += 16
                s.dma_start(out=wob_rowf[0:1, :], in_=b_in["wob"][None, :]
                            ).then_inc(SC["ldm"].sem, 16)
                SC["ldm"].n += 16
            ldm_all2 = SC["ldm"].n

            @block.vector
            def _(v):
                v.wait_ge(SC["ldm"].sem, ldm_all2)
                v.memset(onesf[:], 1.0)
                v.memset(onesb[:], 1.0)
                v.memset(minf_sb[:], -1e30)
                v.tensor_copy(identb[:], identf[:])
                v.tensor_copy(selb[:, :], self_f[:, :])
                v.memset(theta[:, 1:4], 0.0)
                v.tensor_copy(wob_row[0:1, :], wob_rowf[0:1, :])
                SC["vx"].inc(v.engine_nop())
            vx0 = SC["vx"].n

            @block.tensor
            def _(t):
                t.wait_ge(SC["vx"].sem, vx0)
                SC["tp"].inc(t.transpose(psum[0:128, 0:1], pos_row[0:1, 0:128],
                                         identf[0:1, 0:1]))
                t.matmul(bank(2), onesf[0:1, 0:128], buv_row[0:1, 0:512],
                         start=True, stop=True, tile_position=(0, 0))
                SC["mm"].inc(
                    t.matmul(bank(3), onesf[0:1, 0:128],
                             buv_row[0:1, 512:1024],
                             start=True, stop=True, tile_position=(0, 0)))

            @block.vector
            def _(v):
                v.wait_ge(SC["tp"].sem, SC["tp"].n)
                v.tensor_copy(iota_p[:, 0:1], psum[0:128, 0:1])
                v.wait_ge(SC["mm"].sem, SC["mm"].n)
                v.tensor_copy(buv_bc[:, 0:512], bank(2))
                v.tensor_copy(buv_bc[:, 512:1024], bank(3))
                SC["vx"].inc(v.engine_nop())
        vx1 = SC["vx"].n

        @block.scalar
        def _(s):
            s.wait_ge(SC["vx"].sem, vx1)
            s.activation(theta[:, 0:1], iota_p[:, 0:1], AF.Exp,
                         scale=LN2_10K)
            s.wait_ge(SC["spc"].sem, SC["spc"].n)
            for _sp in range(7):
                ins = s.copy(spacer[0:1, _sp:_sp + 1], onesf[0:1, 0:1])
            SC["spc"].inc(ins)
            SC["sc"].inc(s.copy(spacer[0:1, 7:8], onesf[0:1, 0:1]))
        sc0 = SC["sc"].n

        @block.vector
        def _(v):
            v.wait_ge(SC["sc"].sem, sc0)
            SC["vx"].inc(v.engine_nop())
        theta_ready = SC["vx"].n

        cast_list = ["wdq", "wdkv", "wuq", "wqr", "wkr", "wuk", "wuv", "wo",
                     "wgt"]
        castP_list = ["w1sts", "w3sts", "w2st", "w1t", "w3t", "w2t"]
        castA_set = set(cast_list)
        cast_done = {}
        with contextlib.ExitStack() as p0:
            scope_barrier()
            CW = 2048
            cf = [p0.enter_context(sbt(f"c_f{i}", [128, CW], F32))
                  for i in range(6)]
            cb = [p0.enter_context(sbt(f"c_b{i}", [128, CW], BF16))
                  for i in range(6)]
            tiles = []
            for wname in cast_list:
                r_, c_ = w_dims[wname]
                total = (r_ // 128) * c_
                for c0 in range(0, total, CW):
                    tiles.append((wname, c0, min(CW, total - c0)))

            cslot = ("cldA", "cldB", "cldC", "cldD", "cldE", "cldF")
            wslot_ = ("spwA", "spwB", "spwC", "spwD", "spwE", "spwF")

            @block.sync
            def _(s):
                for i, (wname, c0, w) in enumerate(tiles):
                    if i >= 6:
                        s.wait_ge(SC["cast"].sem, i - 5)
                    s.dma_start(out=cf[i % 6][:, 0:w],
                                in_=w_in[wname][0:128, c0:c0 + w]
                                ).then_inc(SC[cslot[i % 6]].sem, 16)
                    SC[cslot[i % 6]].n += 16

            @block.scalar
            def _(sc):
                for i, (wname, c0, w) in enumerate(tiles):
                    sc.wait_ge(SC[cslot[i % 6]].sem, 16 * (i // 6 + 1))
                    if i >= 6:
                        sc.wait_ge(SC[wslot_[i % 6]].sem, 16 * (i // 6))
                    SC["cast"].inc(sc.activation(cb[i % 6][:, 0:w],
                                                 cf[i % 6][:, 0:w],
                                                 AF.Identity))

            @block.gpsimd
            def _(g):
                for i, (wname, c0, w) in enumerate(tiles):
                    g.wait_ge(SC["cast"].sem, i + 1)
                    g.dma_start(out=wscr[wname][0:128, c0:c0 + w],
                                in_=cb[i % 6][:, 0:w]
                                ).then_inc(SC[wslot_[i % 6]].sem, 16)
                    SC[wslot_[i % 6]].n += 16
        spw_totals = tuple((nm_, SC[nm_].n) for nm_ in (
            "spwA", "spwB", "spwC", "spwD", "spwE", "spwF"))

        pump_bufs = {"cf": None, "cb": None}
        pump_tiles_list = []
        for wname in castP_list:
            r_, c_ = w_dims[wname]
            total_ = (r_ // 128) * c_
            for c0_ in range(0, total_, 1024):
                pump_tiles_list.append((wname, c0_, min(1024, total_ - c0_)))
        pump_state = {"i": 0}
        cldP_ = ("cldP0", "cldP1", "cldP2", "cldP3")
        spwP_ = ("spwP0", "spwP1", "spwP2", "spwP3")

        def pump_tiles(n):
            i0 = pump_state["i"]
            i1 = min(i0 + n, len(pump_tiles_list))
            pump_state["i"] = i1
            if i1 == i0:
                return

            cfP = pump_bufs["cf"]
            cbP = pump_bufs["cb"]

            @block.sync
            def _(s):
                for i in range(i0, i1):
                    wname, c0, w = pump_tiles_list[i]
                    if i >= 4:
                        s.wait_ge(SC[spwP_[i % 4]].sem, 16 * (i // 4))
                    s.dma_start(out=cfP[i % 4][:, 0:w],
                                in_=w_in[wname][0:128, c0:c0 + w]
                                ).then_inc(SC[cldP_[i % 4]].sem, 16)
                    SC[cldP_[i % 4]].n += 16

            @block.vector
            def _(v):
                for i in range(i0, i1):
                    wname, c0, w = pump_tiles_list[i]
                    v.wait_ge(SC[cldP_[i % 4]].sem, 16 * (i // 4 + 1))
                    if i >= 4:
                        v.wait_ge(SC[spwP_[i % 4]].sem, 16 * (i // 4))
                    SC["castP"].inc(v.tensor_copy(cbP[i % 4][:, 0:w],
                                                  cfP[i % 4][:, 0:w]))

            @block.gpsimd
            def _(g):
                for i in range(i0, i1):
                    wname, c0, w = pump_tiles_list[i]
                    g.wait_ge(SC["castP"].sem, i + 1)
                    g.dma_start(out=wscr[wname][0:128, c0:c0 + w],
                                in_=cbP[i % 4][:, 0:w]
                                ).then_inc(SC[spwP_[i % 4]].sem, 16)
                    SC[spwP_[i % 4]].n += 16

        @block.gpsimd
        def _(g):
            for nm_, cnt_ in spw_totals:
                g.wait_ge(SC[nm_].sem, cnt_)
            g.dma_start(out=wgate_sb[:, :],
                        in_=wscr["wgt"][0:128, :]
                        ).then_inc(SC["ldm2"].sem, 16)
            SC["ldm2"].n += 16
        wgate_ready = SC["ldm2"].n

        def load_weight_rows(s, wname, row0, nrows, dst, ctr, war_wait=True):
            cols = wcols[wname]
            if wname in castA_set:
                for nm_, cnt_ in spw_totals:
                    s.wait_ge(SC[nm_].sem, cnt_)
            else:
                for nm_ in spwP_:
                    if SC[nm_].n:
                        s.wait_ge(SC[nm_].sem, SC[nm_].n)
            if war_wait and SC["mm"].n:
                s.wait_ge(SC["mm"].sem, SC["mm"].n)
            c0 = (row0 // 128) * cols
            nc_ = (nrows // 128) * cols
            s.dma_start(out=dst[:, 0:nc_],
                        in_=wscr[wname][0:128, c0:c0 + nc_]
                        ).then_inc(ctr.sem, 16)
            ctr.n += 16
            return ctr.n

        def load_weight_bf16(s, wname, dst, ctr, war_wait=True):
            rows = w_dims[wname][0]
            return load_weight_rows(s, wname, 0, rows, dst, ctr,
                                    war_wait=war_wait)

        def proj_fm(wname, nm, TW, src, dst_fn, bias_fn, banks, wstate,
                    dst_off=0, skip_vx=False):
            rows = w_dims[wname][0]
            WC = wcols[wname]
            nk = rows // 128
            ngrp = TW // 512
            jobs = [(m, g) for m in range(nm) for g in range(ngrp)]
            if nk * WC <= 4096:
                slot = wstate["i"] % 2
                wstate["i"] += 1
                wbuf = wstate["bufs"][slot]
                war = wstate["marks"][slot]

                @block.sync
                def _(s):
                    if war:
                        s.wait_ge(SC["mm"].sem, war)
                    load_weight_rows(s, wname, 0, rows, wbuf, SC["ldf"]
                                     if slot == 0 else SC["ldfB"],
                                     war_wait=False)
                wsem = SC["ldf"] if slot == 0 else SC["ldfB"]
                wready = wsem.n
                nbk = len(banks)
                mm_base = SC["mm"].n
                ev_base = SC["ev"].n

                @block.tensor
                def _(t):
                    t.wait_ge(wsem.sem, wready)
                    tbar(t, skip_vx=skip_vx)
                    for j, (m, g) in enumerate(jobs):
                        if j >= nbk:
                            t.wait_ge(SC["ev"].sem, ev_base + j - nbk + 1)
                        bk = bank(banks[j % nbk])
                        for k in range(nk):
                            ins = t.matmul(
                                bk,
                                wbuf[:, k * WC + dst_off + m * 128:
                                     k * WC + dst_off + (m + 1) * 128],
                                src[:, k * TW + g * 512:
                                    k * TW + (g + 1) * 512],
                                start=(k == 0), stop=(k == nk - 1))
                        SC["mm"].inc(ins)
                wstate["marks"][slot] = SC["mm"].n

                vx_mark = SC["vx"].n

                @block.scalar
                def _(s):
                    if skip_vx and vx_mark:
                        s.wait_ge(SC["vx"].sem, vx_mark)
                    for j, (m, g) in enumerate(jobs):
                        s.wait_ge(SC["mm"].sem, mm_base + j + 1)
                        bk = bank(banks[j % nbk])
                        kw = {}
                        bt = bias_fn(m)
                        if bt is not None:
                            kw["bias"] = bt
                        SC["ev"].inc(s.activation(
                            dst_fn(m)[:, g * 512:(g + 1) * 512], bk,
                            AF.Identity, **kw))
            else:
                assert len(jobs) <= 8 and nk % 2 == 0
                hk = nk // 2
                mm_base = SC["mm"].n
                wsems = []
                bufs = []
                for wh in range(2):
                    slot = wstate["i"] % 2
                    wstate["i"] += 1
                    wbuf = wstate["bufs"][slot]
                    war = wstate["marks"][slot]
                    sem = SC["ldf"] if slot == 0 else SC["ldfB"]

                    @block.sync
                    def _(s):
                        if war:
                            s.wait_ge(SC["mm"].sem, war)
                        load_weight_rows(s, wname, wh * hk * 128, hk * 128,
                                         wbuf, sem, war_wait=False)
                    wsems.append((sem, sem.n))
                    bufs.append(wbuf)
                    wstate["marks"][slot] = None

                for wh in range(2):
                    sem, wready = wsems[wh]
                    wbuf = bufs[wh]

                    @block.tensor
                    def _(t):
                        t.wait_ge(sem.sem, wready)
                        if wh == 0:
                            tbar(t, skip_vx=skip_vx)
                        for j, (m, g) in enumerate(jobs):
                            for k in range(hk):
                                ins = t.matmul(
                                    bank(j),
                                    wbuf[:, k * WC + dst_off + m * 128:
                                         k * WC + dst_off + (m + 1) * 128],
                                    src[:, (wh * hk + k) * TW + g * 512:
                                        (wh * hk + k) * TW + (g + 1) * 512],
                                    start=(wh == 0 and k == 0),
                                    stop=(wh == 1 and k == hk - 1))
                            SC["mm"].inc(ins)
                for slot in (0, 1):
                    if wstate["marks"][slot] is None:
                        wstate["marks"][slot] = SC["mm"].n

                vx_mark = SC["vx"].n

                @block.scalar
                def _(s):
                    if skip_vx and vx_mark:
                        s.wait_ge(SC["vx"].sem, vx_mark)
                    for j, (m, g) in enumerate(jobs):
                        s.wait_ge(SC["mm"].sem,
                                  mm_base + len(jobs) + j + 1)
                        kw = {}
                        bt = bias_fn(m)
                        if bt is not None:
                            kw["bias"] = bt
                        SC["ev"].inc(s.activation(
                            dst_fn(m)[:, g * 512:(g + 1) * 512], bank(j),
                            AF.Identity, **kw))

        def dump(name, src_ap, shape, dtype=F32):
            if name not in DEBUG_DUMPS:
                return
            d = nc.dram_tensor("dbg_" + name, shape, dtype,
                               kind="ExternalOutput").ap()
            dbg_outs[name] = d

            @block.gpsimd
            def _(g):
                for n in SC:
                    if n != "out" and SC[n].n:
                        g.wait_ge(SC[n].sem, SC[n].n)
                g.dma_start(out=d[:, :], in_=src_ap).then_inc(SC["out"].sem, 16)
                SC["out"].n += 16

        def gen_rope_tables(tbl):
            cos_sb = tbl.enter_context(sbt("cos_sb", [128, 4 * S], BF16))
            sin_sb = tbl.enter_context(sbt("sin_sb", [128, 4 * S], BF16))
            scope_barrier()
            with contextlib.ExitStack() as tg:
                scrfT = tg.enter_context(sbt("scrfT", [128, 2 * S], F32))
                scrgT = tg.enter_context(sbt("scrgT", [128, S], BF16))
                for c in range(4):
                    @block.gpsimd
                    def _(g):
                        if SC["vx"].n:
                            g.wait_ge(SC["vx"].sem, SC["vx"].n)
                        g.dma_start(out=scrfT[:, 0:S], in_=pos_in[:, :]
                                    ).then_inc(SC["ldp"].sem, 16)
                        SC["ldp"].n += 16
                    m0 = SC["ldp"].n

                    @block.scalar
                    def _(s):
                        s.wait_ge(SC["vx"].sem, theta_ready)
                        s.wait_ge(SC["ldp"].sem, m0)
                        SC["sc"].inc(s.activation(
                            scrfT[:, S:2 * S], scrfT[:, 0:S],
                            AF.Identity, scale=theta[:, c:c + 1]))
                    m1 = SC["sc"].n

                    @block.vector
                    def _(v):
                        v.wait_ge(SC["sc"].sem, m1)
                        SC["vx"].inc(v.tensor_scalar(
                            scrfT[:, 0:S], scrfT[:, S:2 * S],
                            512.0 * PI, None, op0=ALU.is_gt))
                        v.wait_ge(SC["vx"].sem, SC["vx"].n)
                        SC["vx"].inc(v.scalar_tensor_tensor(
                            scrfT[:, S:2 * S], scrfT[:, 0:S],
                            -512.0 * PI, scrfT[:, S:2 * S],
                            op0=ALU.mult, op1=ALU.add))
                        v.wait_ge(SC["vx"].sem, SC["vx"].n)
                        SC["vx"].inc(v.tensor_scalar(
                            scrfT[:, 0:S], scrfT[:, 0:S],
                            -2.0, 1.0, op0=ALU.mult, op1=ALU.add))
                    m2 = SC["vx"].n

                    @block.scalar
                    def _(s):
                        s.wait_ge(SC["vx"].sem, m2)
                        SC["sc"].inc(s.activation(
                            scrgT[:, 0:S], scrfT[:, S:2 * S],
                            AF.Sin, scale=1.0 / 512.0))
                    m3 = SC["sc"].n

                    @block.vector
                    def _(v):
                        v.wait_ge(SC["sc"].sem, m3)
                        v.tensor_tensor(sin_sb[:, c * S:(c + 1) * S],
                                        scrgT[:, 0:S], scrfT[:, 0:S],
                                        op=ALU.mult)
                        SC["vx"].inc(v.tensor_scalar(
                            scrfT[:, S:2 * S], scrfT[:, S:2 * S],
                            -1.0, 512.0 * PI / 2,
                            op0=ALU.mult, op1=ALU.add))
                    m4 = SC["vx"].n

                    @block.scalar
                    def _(s):
                        s.wait_ge(SC["vx"].sem, m4)
                        SC["sc"].inc(s.activation(
                            scrgT[:, 0:S], scrfT[:, S:2 * S],
                            AF.Sin, scale=1.0 / 512.0))
                    m5 = SC["sc"].n

                    @block.vector
                    def _(v):
                        v.wait_ge(SC["sc"].sem, m5)
                        SC["vx"].inc(v.tensor_tensor(
                            cos_sb[:, c * S:(c + 1) * S],
                            scrgT[:, 0:S], scrfT[:, 0:S], op=ALU.mult))

            return cos_sb, sin_sb

        for seq in range(2):
            with contextlib.ExitStack() as sqs:
                qEA = sqs.enter_context(sbt("qEA", [128, 4 * S], BF16))
                qOB = sqs.enter_context(sbt("qOB", [128, 4 * S], BF16))
                kEA = sqs.enter_context(sbt("kEA", [128, 4 * S], BF16))
                kOB = sqs.enter_context(sbt("kOB", [128, 4 * S], BF16))
                v_sb = sqs.enter_context(
                    sbt("v_sb", [128, 16 * 1024], BF16))
                qkmax = sqs.enter_context(sbt("qkmax", [128, 40], F32))
                nqkm = [0]
                tblsc = contextlib.ExitStack()
                cos_sb, sin_sb = gen_rope_tables(tblsc)

                for half in range(2):
                    TW = 1024
                    tok0 = seq * S + half * TW
                    with contextlib.ExitStack() as ph:
                        xnT = ph.enter_context(
                            sbt("xnT", [128, 8 * TW], BF16))
                        ct_q = ph.enter_context(
                            sbt("ct_q", [128, 4 * TW], BF16))
                        ct_kv = ph.enter_context(
                            sbt("ct_kv", [128, 4 * TW], BF16))
                        scrf = ph.enter_context(
                            sbt("scrf", [128, TW], F32))
                        scrg = ph.enter_context(
                            sbt("scrg", [128, TW], BF16))

                        scope_barrier()
                        with contextlib.ExitStack() as p1:
                            scope_barrier()
                            xg = [p1.enter_context(
                                sbt(f"xg{i}", [128, 1024], F32))
                                for i in range(4)]
                            stats = p1.enter_context(
                                sbt("stats", [128, 32], F32))
                            xn_tok = scrg
                            sco = scrf[0:16, 0:TW]
                            for grp in range(2):
                                sb0 = 16 * (grp % 2)
                                lsem = SC["ldxA"] if grp % 2 == 0 else SC["ldxB"]
                                sc_before = SC["sc"].n
                                vx_before = SC["vx"].n

                                @block.sync
                                def _(s):
                                    if grp > 0 or half > 0 or seq > 0:
                                        s.wait_ge(SC["sc"].sem, sc_before)
                                        s.wait_ge(SC["vx"].sem, vx_before)
                                    for i in range(4):
                                        tl = grp * 4 + i
                                        trow = tok0 + tl * 128
                                        s.dma_start(
                                            out=xg[i][:],
                                            in_=x_in[trow:trow + 128, :]
                                        ).then_inc(lsem.sem, 16)
                                        lsem.n += 16
                                g_all = lsem.n

                                @block.vector
                                def _(v):
                                    v.wait_ge(lsem.sem, g_all)
                                    for i in range(4):
                                        SC["vx"].inc(v.reduce_sum(
                                            stats[:, sb0 + i:sb0 + i + 1],
                                            xg[i][:], axis=AXX))
                                vx_s = SC["vx"].n

                                @block.scalar
                                def _(s):
                                    s.wait_ge(SC["tp"].sem, SC["tp"].n)
                                    s.wait_ge(lsem.sem, g_all)
                                    for i in range(4):
                                        if i:
                                            s.wait_ge(SC["sc"].sem, SC["sc"].n)
                                        SC["sc"].inc(s.activation(
                                            xn_tok[:], xg[i][:], AF.Square,
                                            accum_out=stats[:, sb0 + 4 + i:sb0 + 5 + i]))
                                sc_s = SC["sc"].n

                                @block.vector
                                def _(v):
                                    v.wait_ge(SC["vx"].sem, vx_s)
                                    v.wait_ge(SC["sc"].sem, sc_s)

                                    def ch(ins):
                                        SC["vx"].inc(ins)
                                        v.wait_ge(SC["vx"].sem, SC["vx"].n)
                                    ch(v.tensor_scalar_mul(
                                        stats[:, sb0 + 0:sb0 + 4],
                                        stats[:, sb0 + 0:sb0 + 4], 1.0 / D))
                                    ch(v.tensor_scalar_mul(
                                        stats[:, sb0 + 4:sb0 + 8],
                                        stats[:, sb0 + 4:sb0 + 8], 1.0 / D))
                                    ch(v.tensor_tensor(
                                        stats[:, sb0 + 8:sb0 + 12],
                                        stats[:, sb0 + 0:sb0 + 4],
                                        stats[:, sb0 + 0:sb0 + 4], op=ALU.mult))
                                    ch(v.tensor_tensor(
                                        stats[:, sb0 + 4:sb0 + 8],
                                        stats[:, sb0 + 4:sb0 + 8],
                                        stats[:, sb0 + 8:sb0 + 12],
                                        op=ALU.subtract))
                                    SC["vx"].inc(v.tensor_scalar_add(
                                        stats[:, sb0 + 4:sb0 + 8],
                                        stats[:, sb0 + 4:sb0 + 8], 1e-5))
                                vx_v = SC["vx"].n

                                @block.scalar
                                def _(s):
                                    s.wait_ge(SC["vx"].sem, vx_v)
                                    SC["sc"].inc(s.activation(
                                        stats[:, sb0 + 8:sb0 + 12], stats[:, sb0 + 4:sb0 + 8],
                                        AF.Sqrt))
                                sc_v = SC["sc"].n

                                @block.vector
                                def _(v):
                                    v.wait_ge(SC["sc"].sem, sc_v)
                                    SC["vx"].inc(v.reciprocal(
                                        stats[:, sb0 + 8:sb0 + 12],
                                        stats[:, sb0 + 8:sb0 + 12]))
                                vx_r = SC["vx"].n

                                for i in range(4):
                                    tl = grp * 4 + i

                                    @block.vector
                                    def _(v):
                                        v.wait_ge(SC["vx"].sem, vx_r)
                                        v.wait_ge(SC["tp"].sem, SC["tp"].n)
                                        SC["vx"].inc(v.tensor_tensor(
                                            xg[i][:], xg[i][:],
                                            stats[:, sb0 + i:sb0 + i + 1]
                                            .to_broadcast([128, 1024]),
                                            op=ALU.subtract))
                                        v.wait_ge(SC["vx"].sem, SC["vx"].n)
                                        SC["vx"].inc(v.tensor_tensor(
                                            xn_tok[:], xg[i][:],
                                            stats[:, sb0 + 8 + i:sb0 + 9 + i]
                                            .to_broadcast([128, 1024]),
                                            op=ALU.mult))
                                    sc_xn = SC["vx"].n

                                    bkb = bank(tl % 2).bitcast(BF16)

                                    @block.tensor
                                    def _(t):
                                        t.wait_ge(SC["vx"].sem, sc_xn)
                                        if SC["ev"].n:
                                            t.wait_ge(SC["ev"].sem,
                                                      SC["ev"].n - 1)
                                        for c in range(8):
                                            SC["tp"].inc(t.transpose(
                                                bkb[:, c * 128:(c + 1) * 128],
                                                xn_tok[:, c * 128:(c + 1) * 128],
                                                identb[:]))
                                    tp_xn = SC["tp"].n

                                    @block.vector
                                    def _(v):
                                        v.wait_ge(SC["tp"].sem, tp_xn)
                                        SC["ev"].inc(v.tensor_copy(
                                            xnT.rearrange(
                                                "p (c w) -> p c w", w=TW)
                                            [:, :, tl * 128:(tl + 1) * 128],
                                            bkb.rearrange(
                                                "p (c w) -> p c w", w=128)))
                            ev_xn = SC["ev"].n

                            @block.gpsimd
                            def _(g):
                                g.wait_ge(SC["ev"].sem, ev_xn)
                                g.dma_start(
                                    out=xnspill[:, tok0:tok0 + TW]
                                    .rearrange("(c p) t -> p c t", p=128),
                                    in_=xnT.rearrange("p (c t) -> p c t", t=TW)
                                    ).then_inc(SC["spx"].sem, 16)
                                SC["spx"].n += 16

                            mm2_base = SC["mm2"].n

                            @block.tensor
                            def _(t):
                                t.wait_ge(SC["ldm2"].sem, wgate_ready)
                                t.wait_ge(SC["ev"].sem, ev_xn)
                                for g in range(TW // 512):
                                    for k in range(8):
                                        ins = t.matmul(
                                            bank(2 + g)[0:16, :],
                                            wgate_sb[:, k * NE:(k + 1) * NE],
                                            xnT[:, k * TW + g * 512:
                                                k * TW + (g + 1) * 512],
                                            start=(k == 0), stop=(k == 7))
                                    SC["mm2"].inc(ins)
                            mm_gs = SC["mm2"].n

                            @block.vector
                            def _(v):
                                v.wait_ge(SC["mm2"].sem, mm_gs)
                                for g in range(TW // 512):
                                    SC["vx"].inc(v.tensor_copy(
                                        sco[0:16, g * 512:(g + 1) * 512],
                                        bank(2 + g)[0:16, :]))
                            vx_gs = SC["vx"].n

                            for tl in range(8):
                                @block.tensor
                                def _(t):
                                    t.wait_ge(SC["vx"].sem, vx_gs)
                                    if SC["ev2"].n:
                                        t.wait_ge(SC["ev2"].sem, SC["ev2"].n)
                                    SC["tp"].inc(t.transpose(
                                        bank(6)[0:128, 0:16],
                                        sco[0:16, tl * 128:(tl + 1) * 128],
                                        identf[0:16, 0:16]))
                                tpn = SC["tp"].n

                                @block.scalar
                                def _(s):
                                    s.wait_ge(SC["tp"].sem, tpn)
                                    gt = (tok0 // 128) + tl
                                    SC["ev2"].inc(s.activation(
                                        scores_tok[:, gt * NE:(gt + 1) * NE],
                                        bank(6)[0:128, 0:16], AF.Sigmoid))

                        if seq == 0 and half == 0:
                            dump("xnT_h0", xnT[:, :], [128, 8 * TW])
                            dump("hook_h0", xnT[:, 0:1], [128, 1])

                        rpA = ph.enter_context(
                            sbt("rpA", [128, 2 * TW], BF16))
                        rpB = ph.enter_context(
                            sbt("rpB", [128, 2 * TW], BF16))
                        wbufA = ph.enter_context(
                            sbt("wbufA", [128, 4096], BF16))
                        wbufB = ph.enter_context(
                            sbt("wbufB", [128, 4096], BF16))
                        wstate = {"bufs": [wbufA, wbufB], "i": 0,
                                  "marks": {0: 0, 1: 0}}
                        scope_barrier()

                        proj_fm("wdq", 4, TW, xnT,
                                lambda m: ct_q[:, m * TW:(m + 1) * TW],
                                lambda m: bias_sb["bdq"][:, m:m + 1],
                                [0, 1, 2], wstate)
                        proj_fm("wdkv", 4, TW, xnT,
                                lambda m: ct_kv[:, m * TW:(m + 1) * TW],
                                lambda m: bias_sb["bdkv"][:, m:m + 1],
                                [3, 4, 5], wstate)

                        def rope_side(wC, srcC, wR, srcR, bCA, bCB, bRA, bRB,
                                      dstA, dstB):
                            for AB, dA, bb in ((0, dstA, bCA), (1, dstB, bCB)):
                                proj_fm(wC, 4, TW, srcC,
                                        lambda m, dA=dA: dA[
                                            :, m * S + half * TW:
                                            m * S + half * TW + TW],
                                        lambda m, bb=bb: bias_sb[bb][:, m:m + 1],
                                        [0, 1, 2], wstate, dst_off=AB * 512,
                                        skip_vx=True)
                            for cp in range(2):
                                for AB, dP, bb in ((0, rpA, bRA), (1, rpB, bRB)):
                                    proj_fm(wR, 2, TW, srcR,
                                            lambda m, dP=dP: dP[
                                                :, m * TW:(m + 1) * TW],
                                            lambda m, bb=bb, cp=cp:
                                            bias_sb[bb][:, 2 * cp + m:
                                                        2 * cp + m + 1],
                                            [3, 4, 5] if wR != "wkr" else None,
                                            wstate,
                                            dst_off=AB * 512 + cp * 256,
                                            skip_vx=True)
                                ev_pre = SC["ev"].n

                                @block.vector
                                def _(v):
                                    v.wait_ge(SC["ev"].sem, ev_pre)
                                    v.wait_ge(SC["vx"].sem, SC["vx"].n)
                                    v.wait_ge(SC["tp"].sem, SC["tp"].n)

                                    def ch(ins):
                                        SC["vx"].inc(ins)
                                        v.wait_ge(SC["vx"].sem, SC["vx"].n)
                                    for cc in range(2):
                                        c = 2 * cp + cc
                                        aa = rpA[:, cc * TW:(cc + 1) * TW]
                                        bbx = rpB[:, cc * TW:(cc + 1) * TW]
                                        co = cos_sb[:, c * S + half * TW:
                                                    c * S + half * TW + TW]
                                        si = sin_sb[:, c * S + half * TW:
                                                    c * S + half * TW + TW]
                                        dAx = dstA[:, c * S + half * TW:
                                                   c * S + half * TW + TW]
                                        dBx = dstB[:, c * S + half * TW:
                                                   c * S + half * TW + TW]
                                        s1 = scrg[:, 0:TW]
                                        s2 = scrf[:, 0:TW]
                                        ch(v.tensor_tensor(s1, aa, co,
                                                           op=ALU.mult))
                                        ch(v.tensor_tensor(s2, bbx, si,
                                                           op=ALU.mult))
                                        ch(v.tensor_tensor(s1, s1, s2,
                                                           op=ALU.subtract))
                                        ch(v.tensor_tensor(dAx, s1, dAx,
                                                           op=ALU.add))
                                        ch(v.tensor_tensor(s1, bbx, co,
                                                           op=ALU.mult))
                                        ch(v.tensor_tensor(s2, aa, si,
                                                           op=ALU.mult))
                                        ch(v.tensor_tensor(s1, s1, s2,
                                                           op=ALU.add))
                                        ch(v.tensor_tensor(dBx, s1, dBx,
                                                           op=ALU.add))
                            ev_pre2 = SC["ev"].n

                            @block.vector
                            def _(v):
                                v.wait_ge(SC["ev"].sem, ev_pre2)
                                v.wait_ge(SC["vx"].sem, SC["vx"].n)
                                for c in range(4):
                                    v.reduce_max(
                                        qkmax[:, nqkm[0] + c:nqkm[0] + c + 1],
                                        dstA[:, c * S + half * TW:
                                             c * S + half * TW + TW],
                                        axis=AXX, apply_absolute_value=True)
                                for c in range(4):
                                    v.reduce_max(
                                        qkmax[:, nqkm[0] + 4 + c:
                                              nqkm[0] + 5 + c],
                                        dstB[:, c * S + half * TW:
                                             c * S + half * TW + TW],
                                        axis=AXX, apply_absolute_value=True)
                                SC["vx"].inc(v.engine_nop())
                            nqkm[0] += 8

                        rope_side("wuk", ct_kv, "wkr", xnT, "bukA", "bukB",
                                  "bkrA", "bkrB", kEA, kOB)
                        rope_side("wuq", ct_q, "wqr", ct_q, "buqA", "buqB",
                                  "bqrA", "bqrB", qEA, qOB)

                        vslot = wstate["i"] % 2
                        wstate["i"] += 1
                        vbuf = wstate["bufs"][vslot]
                        vwar = wstate["marks"][vslot]
                        vsem = SC["ldf"] if vslot == 0 else SC["ldfB"]

                        @block.sync
                        def _(s):
                            if vwar:
                                s.wait_ge(SC["mm"].sem, vwar)
                            load_weight_bf16(s, "wuv", vbuf, vsem,
                                             war_wait=False)
                        wready = vsem.n
                        mm_base = SC["mm"].n
                        ev_base = SC["ev"].n

                        @block.tensor
                        def _(t):
                            t.wait_ge(vsem.sem, wready)
                            tbar(t, skip_vx=True)
                            for tl in range(8):
                                if tl >= 3:
                                    t.wait_ge(SC["ev"].sem,
                                              ev_base + 2 * (tl - 2))
                                b0 = bank(2 * (tl % 3))
                                b1 = bank(2 * (tl % 3) + 1)
                                for k in range(4):
                                    t.matmul(b0,
                                             ct_kv[:, k * TW + tl * 128:
                                                   k * TW + (tl + 1) * 128],
                                             vbuf[:, k * 1024:k * 1024 + 512],
                                             start=(k == 0), stop=(k == 3))
                                    ins = t.matmul(
                                        b1,
                                        ct_kv[:, k * TW + tl * 128:
                                              k * TW + (tl + 1) * 128],
                                        vbuf[:, k * 1024 + 512:
                                              (k + 1) * 1024],
                                        start=(k == 0), stop=(k == 3))
                                SC["mm"].inc(ins)
                        wstate["marks"][vslot] = SC["mm"].n

                        @block.vector
                        def _(v):
                            for tl in range(8):
                                v.wait_ge(SC["mm"].sem, mm_base + tl + 1)
                                vt = half * 8 + tl
                                b0 = bank(2 * (tl % 3))
                                b1 = bank(2 * (tl % 3) + 1)
                                v.scalar_tensor_tensor(
                                    v_sb[:, vt * 1024:vt * 1024 + 512],
                                    b0, 0.0, buv_bc[:, 0:512],
                                    op0=ALU.bypass, op1=ALU.add)
                                SC["ev"].inc(v.scalar_tensor_tensor(
                                    v_sb[:, vt * 1024 + 512:(vt + 1) * 1024],
                                    b1, 0.0, buv_bc[:, 512:1024],
                                    op0=ALU.bypass, op1=ALU.add), 2)

                    if seq == 0 and half == 0:
                        dump("ctq_h0", ct_q[:, :], [128, 4 * TW])
                        dump("ctkv_h0", ct_kv[:, :], [128, 4 * TW])
                        dump("cos_h0", cos_sb[:, :], [128, 4 * TW], dtype=BF16)
                        dump("sin_h0", sin_sb[:, :], [128, 4 * TW], dtype=BF16)
                        dump("rpA_h0", rpA[:, :], [128, 4 * TW], dtype=BF16)
                        dump("rpB_h0", rpB[:, :], [128, 4 * TW], dtype=BF16)

                if seq == 0:
                    dump("qEA_s0", qEA[:, :], [128, 4 * S])
                    dump("qOB_s0", qOB[:, :], [128, 4 * S])
                    dump("kEA_s0", kEA[:, :], [128, 4 * S])
                    dump("v_s0", v_sb[:, :], [128, 16 * 1024])

                if BUILD_STAGE < 2:
                    continue

                @block.vector
                def _(v):
                    v.wait_ge(SC["vx"].sem, SC["vx"].n)
                    SC["vx"].inc(v.reduce_max(qkmax[:, 32:33],
                                              qkmax[:, 16:32], axis=AXX))
                    v.wait_ge(SC["vx"].sem, SC["vx"].n)
                    SC["vx"].inc(v.reduce_max(qkmax[:, 33:34],
                                              qkmax[:, 0:16], axis=AXX))
                vx_m = SC["vx"].n

                @block.tensor
                def _(t):
                    t.wait_ge(SC["vx"].sem, vx_m)
                    tbar(t)
                    t.transpose(bank(0)[0:1, 0:128], qkmax[:, 32:33],
                                identf[:])
                    SC["tp"].inc(t.transpose(bank(1)[0:1, 0:128],
                                             qkmax[:, 33:34], identf[:]))
                tp_m = SC["tp"].n

                @block.vector
                def _(v):
                    v.wait_ge(SC["tp"].sem, tp_m)
                    SC["vx"].inc(v.reduce_max(qkmax[0:1, 34:35],
                                              bank(0)[0:1, 0:128], axis=AXX))
                    v.wait_ge(SC["vx"].sem, SC["vx"].n)
                    SC["vx"].inc(v.reduce_max(qkmax[0:1, 35:36],
                                              bank(1)[0:1, 0:128], axis=AXX))
                    v.wait_ge(SC["vx"].sem, SC["vx"].n)
                    SC["vx"].inc(v.tensor_tensor(
                        qkmax[0:1, 36:37], qkmax[0:1, 34:35],
                        qkmax[0:1, 35:36], op=ALU.mult))
                vx_m2 = SC["vx"].n

                @block.tensor
                def _(t):
                    t.wait_ge(SC["vx"].sem, vx_m2)
                    SC["mm"].inc(t.matmul(
                        bank(2)[:, 0:1], onesf[0:1, 0:128],
                        qkmax[0:1, 36:37], start=True, stop=True,
                        tile_position=(0, 0)))
                mm_m = SC["mm"].n

                @block.scalar
                def _(s):
                    s.wait_ge(SC["mm"].sem, mm_m)
                    s.activation(expM[:, 0:1], bank(2)[:, 0:1],
                                 AF.Identity, scale=-64.0 / SCALE_C)
                    s.wait_ge(SC["spc"].sem, SC["spc"].n)
                    for _sp in range(7):
                        ins = s.copy(spacer[0:1, _sp:_sp + 1], onesf[0:1, 0:1])
                    SC["spc"].inc(ins)
                    SC["sc"].inc(s.copy(spacer[0:1, 7:8], onesf[0:1, 0:1]))
                sc_m = SC["sc"].n

                if BUILD_STAGE < 3:
                    continue

                tblsc.close()
                QB = 256
                with contextlib.ExitStack() as ae:
                    scope_barrier()
                    pT = ae.enter_context(
                        sbt("pT", [128, 4 * 16 * QB], BF16))
                    attnT = ae.enter_context(
                        sbt("attnT", [128, 8 * QB], BF16))
                    attnF = ae.enter_context(
                        sbt("attnF", [128, 8 * QB], BF16))
                    zrow = ae.enter_context(
                        sbt("zrow", [128, QB], F32))

                    @block.vector
                    def _(v):
                        SC["vx"].inc(v.memset(zrow[:, :], 1.0))
                    rzbc = ae.enter_context(
                        sbt("rzbc", [128, 2 * QB], F32))
                    atok = ae.enter_context(
                        sbt("atok", [128, 2 * 1024], BF16))
                    wo_sb = ae.enter_context(
                        sbt("wo_sb", [128, 8 * 1024], BF16))
                    pump_bufs["cf"] = [ae.enter_context(
                        sbt(f"cP_f{i}", [128, 1024], F32)) for i in range(4)]
                    pump_bufs["cb"] = [ae.enter_context(
                        sbt(f"cP_b{i}", [128, 1024], BF16)) for i in range(4)]

                    @block.sync
                    def _(s):
                        load_weight_bf16(s, "wo", wo_sb, SC["ldf2"])
                    wo_ready = SC["ldf2"].n

                    for qb in range(S // QB):
                        q0 = qb * QB
                        for quad in range(4):
                            exp_base = SC["exp"].n
                            mm_base = SC["mm"].n

                            @block.tensor
                            def _(t):
                                t.wait_ge(SC["sc"].sem, sc_m)
                                tbar(t)
                                for kc in range(16):
                                    p = kc // 2
                                    if kc % 2 == 0 and p >= 2:
                                        t.wait_ge(SC["exp"].sem,
                                                  exp_base
                                                  + 4 * ((p - 2) // 2 + 1))
                                    for j in range(4):
                                        bk = bank(2 * j + p % 2)[
                                            :, (kc % 2) * QB:
                                            (kc % 2) * QB + QB]
                                        t.matmul(
                                            bk,
                                            kEA[32 * j:32 * (j + 1),
                                                quad * S + kc * 128:
                                                quad * S + (kc + 1) * 128],
                                            qEA[32 * j:32 * (j + 1),
                                                quad * S + q0:
                                                quad * S + q0 + QB],
                                            start=(kc % 2 == 0), stop=False,
                                            tile_position=(32 * j, 0))
                                        ins = t.matmul(
                                            bk,
                                            kOB[32 * j:32 * (j + 1),
                                                quad * S + kc * 128:
                                                quad * S + (kc + 1) * 128],
                                            qOB[32 * j:32 * (j + 1),
                                                quad * S + q0:
                                                quad * S + q0 + QB],
                                            start=False, stop=(kc % 2 == 1),
                                            tile_position=(32 * j, 0))
                                        SC["mm"].inc(ins)

                            @block.scalar
                            def _(s):
                                s.wait_ge(SC["sc"].sem, sc_m)
                                for kcg in range(4):
                                    for j in range(4):
                                        s.wait_ge(SC["mm"].sem,
                                                  mm_base
                                                  + 4 * (4 * kcg + 3) + j + 1)
                                        SC["exp"].inc(s.activation(
                                            pT[:, (j * 16 + 4 * kcg) * QB:
                                               (j * 16 + 4 * kcg + 4) * QB],
                                            psum[:, (2 * j) * 512:
                                                 (2 * j + 2) * 512],
                                            AF.Exp, bias=expM[:, 0:1],
                                            scale=1.0 / SCALE_C))
                            exp_all = SC["exp"].n

                            @block.tensor
                            def _(t):
                                t.wait_ge(SC["exp"].sem, exp_all)
                                for kc in range(16):
                                    st = (kc == 0)
                                    sp = (kc == 15)
                                    for j in range(4):
                                        h = 4 * quad + j
                                        ins = t.matmul(
                                            bank(j)[64 * (j % 2):
                                                    64 * (j % 2) + 64, 0:QB],
                                            v_sb[:, kc * 1024 + 64 * h:
                                                 kc * 1024 + 64 * h + 64],
                                            pT[:, (j * 16 + kc) * QB:
                                               (j * 16 + kc + 1) * QB],
                                            start=st, stop=sp,
                                            tile_position=(0, 64 * (j % 2)))
                                        if sp:
                                            SC["pvm"].inc(ins)
                                    for j in range(4):
                                        ins = t.matmul(
                                            bank(4)[32 * j:32 * j + 1, 0:QB],
                                            onesb[:, 0:1],
                                            pT[:, (j * 16 + kc) * QB:
                                               (j * 16 + kc + 1) * QB],
                                            start=st, stop=sp,
                                            tile_position=(0, 32 * j))
                                        if sp:
                                            SC["pvm"].inc(ins)
                            pv_all = SC["pvm"].n

                            @block.vector
                            def _(v):
                                v.wait_ge(SC["pvm"].sem, pv_all)
                                for j in range(4):
                                    SC["vx"].inc(v.tensor_copy(
                                        zrow[32 * j:32 * j + 1, :],
                                        bank(4)[32 * j:32 * j + 1, 0:QB]))
                                v.wait_ge(SC["vx"].sem, SC["vx"].n)
                                SC["vx"].inc(v.reciprocal(zrow[:, :],
                                                          zrow[:, :]))
                            vx_z = SC["vx"].n
                            mmb_base = SC["mm2"].n

                            @block.tensor
                            def _(t):
                                t.wait_ge(SC["vx"].sem, vx_z)
                                for j in range(4):
                                    SC["mm2"].inc(t.matmul(
                                        bank(5 + j // 2)[
                                            64 * (j % 2):64 * (j % 2) + 64,
                                            0:QB],
                                        onesf[32 * j:32 * j + 1, 0:64],
                                        zrow[32 * j:32 * j + 1, :],
                                        start=True, stop=True,
                                        tile_position=(32 * j, 64 * (j % 2))))

                            mm_cur = SC["mm"].n

                            @block.vector
                            def _(v):
                                v.wait_ge(SC["mm"].sem, mm_cur)
                                for j in range(4):
                                    v.wait_ge(SC["mm2"].sem, mmb_base + j + 1)
                                    SC["vx"].inc(v.tensor_copy(
                                        rzbc[64 * (j % 2):64 * (j % 2) + 64,
                                             (j // 2) * QB:
                                             (j // 2 + 1) * QB],
                                        bank(5 + j // 2)[
                                            64 * (j % 2):64 * (j % 2) + 64,
                                            0:QB]))
                                    v.wait_ge(SC["vx"].sem, SC["vx"].n)
                                    h = 4 * quad + j
                                    pp = 64 * (j % 2)
                                    SC["oev"].inc(v.tensor_tensor(
                                        attnT[pp:pp + 64,
                                              (h // 2) * QB:
                                              (h // 2 + 1) * QB],
                                        bank(j)[pp:pp + 64, 0:QB],
                                        rzbc[pp:pp + 64,
                                             (j // 2) * QB:
                                             (j // 2 + 1) * QB],
                                        op=ALU.mult))

                        oev_all = SC["oev"].n
                        mm_base = SC["mm"].n
                        ev_base = SC["ev"].n

                        @block.tensor
                        def _(t):
                            t.wait_ge(SC["ldf2"].sem, wo_ready)
                            t.wait_ge(SC["oev"].sem, oev_all)
                            tbar(t)
                            for m in range(8):
                                if m >= 3:
                                    t.wait_ge(SC["ev"].sem, ev_base + m - 2)
                                for k in range(8):
                                    ins = t.matmul(
                                        bank(m % 3)[:, 0:QB],
                                        wo_sb[:, k * 1024 + m * 128:
                                              k * 1024 + (m + 1) * 128],
                                        attnT[:, k * QB:(k + 1) * QB],
                                        start=(k == 0), stop=(k == 7))
                                SC["mm"].inc(ins)

                        tp_prev = SC["tp"].n

                        @block.scalar
                        def _(s):
                            s.wait_ge(SC["tp"].sem, tp_prev)
                            for m in range(8):
                                s.wait_ge(SC["mm"].sem, mm_base + m + 1)
                                SC["ev"].inc(s.activation(
                                    attnF[:, m * QB:(m + 1) * QB],
                                    bank(m % 3)[:, 0:QB], AF.Identity))
                        ev_wo = SC["ev"].n

                        ev2_base = SC["ev2"].n
                        tp_marks = []

                        @block.tensor
                        def _(t):
                            t.wait_ge(SC["ev"].sem, ev_wo)
                            for u in range(2):
                                bkb = bank(3 + u).bitcast(BF16)
                                for m in range(8):
                                    SC["tp"].inc(t.transpose(
                                        bkb[:, m * 128:(m + 1) * 128],
                                        attnF[:, m * QB + u * 128:
                                              m * QB + (u + 1) * 128],
                                        identb[:]))
                                tp_marks.append(SC["tp"].n)

                        spx_prev = SC["spx"].n

                        @block.vector
                        def _(v):
                            v.wait_ge(SC["spx"].sem, spx_prev)
                            for u in range(2):
                                v.wait_ge(SC["tp"].sem, tp_marks[u])
                                SC["ev2"].inc(v.tensor_copy(
                                    atok[:, u * 1024:(u + 1) * 1024],
                                    bank(3 + u).bitcast(BF16)))
                        ev_at = SC["ev2"].n

                        @block.gpsimd
                        def _(g):
                            g.wait_ge(SC["ev2"].sem, ev_at)
                            trow = seq * S + q0
                            g.dma_start(
                                out=attnspill[trow:trow + QB, :]
                                .rearrange("(u p) d -> p u d", p=128),
                                in_=atok.rearrange("p (u d) -> p u d", d=1024)
                                ).then_inc(SC["spx"].sem, 16)
                            SC["spx"].n += 16

                        pump_tiles(14 if seq == 0 else 13)

        pump_tiles(len(pump_tiles_list))
        dump("xnspill", xnspill[:, :], [D, T], dtype=BF16)
        dump("attnspill", attnspill[:, :], [T, D], dtype=BF16)
        dump("scores_tok", scores_tok[:, :], [128, 32 * NE])

        if BUILD_STAGE >= 4:
            with contextlib.ExitStack() as gs:
                scope_barrier()
                g1 = gs.enter_context(sbt("g1", [128, 512], F32))
                g2 = gs.enter_context(sbt("g2", [128, 512], F32))
                g3 = gs.enter_context(sbt("g3", [128, 96], F32))
                t8 = gs.enter_context(sbt("t8", [128, 8], F32))
                gw = gs.enter_context(sbt("gw", [128, NE], F32))

                @block.vector
                def _(v):
                    v.wait_ge(SC["ev2"].sem, SC["ev2"].n)

                    def ch(ins):
                        SC["vx"].inc(ins)
                        v.wait_ge(SC["vx"].sem, SC["vx"].n)
                    s4 = scores_tok.rearrange("p (tg j) -> p tg j", j=4)
                    pairs = [(0, 1), (0, 2), (0, 3), (1, 2), (1, 3), (2, 3)]
                    for pi, (a, b) in enumerate(pairs):
                        ch(v.tensor_tensor(g2[:, 0:128], s4[:, :, a],
                                           s4[:, :, b], op=ALU.add))
                        if pi == 0:
                            ch(v.tensor_copy(g1[:, 0:128], g2[:, 0:128]))
                        else:
                            ch(v.tensor_tensor(g1[:, 0:128], g1[:, 0:128],
                                               g2[:, 0:128], op=ALU.max))
                    gg = g1[:, 0:128].rearrange("p (t g) -> p t g", g=4)
                    for pi, (a, b) in enumerate(pairs):
                        ch(v.tensor_tensor(g2[:, 128:160], gg[:, :, a],
                                           gg[:, :, b], op=ALU.min))
                        if pi == 0:
                            ch(v.tensor_copy(g3[:, 0:32], g2[:, 128:160]))
                        else:
                            ch(v.tensor_tensor(g3[:, 0:32], g3[:, 0:32],
                                               g2[:, 128:160], op=ALU.max))
                    thr_b = g3[:, 0:32].unsqueeze(2).to_broadcast([128, 32, 4])
                    ch(v.tensor_tensor(gg, gg, thr_b, op=ALU.is_ge))
                    alw_b = gg.unsqueeze(3).to_broadcast([128, 32, 4, 4])
                    s_4d = scores_tok.rearrange("p (t g j) -> p t g j",
                                                g=4, j=4)
                    m_4d = g2[:, 0:512].rearrange("p (t g j) -> p t g j",
                                                  g=4, j=4)
                    ch(v.tensor_tensor(m_4d, s_4d, alw_b, op=ALU.mult))
                    ch(v.tensor_scalar(g1[:, 128:256].rearrange(
                        "p (t g) -> p t g", g=4), gg, 1e30, -1e30,
                        op0=ALU.mult, op1=ALU.add))
                    neg_b = g1[:, 128:256].rearrange(
                        "p (t g) -> p t g", g=4).unsqueeze(3).to_broadcast(
                        [128, 32, 4, 4])
                    ch(v.tensor_tensor(m_4d, m_4d, neg_b, op=ALU.add))
                    for tl in range(32):
                        ch(v.max(out=t8[:, :],
                                 in_=g2[:, tl * 16:(tl + 1) * 16]))
                        ch(v.tensor_tensor(g3[:, 64:80],
                                           g2[:, tl * 16:(tl + 1) * 16],
                                           t8[:, 3:4].to_broadcast([128, 16]),
                                           op=ALU.is_ge))
                        ch(v.tensor_tensor(
                            gw[:, :], scores_tok[:, tl * 16:(tl + 1) * 16],
                            g3[:, 64:80], op=ALU.mult))
                        ch(v.reduce_sum(g3[:, 80:81], gw[:, :], axis=AXX))
                        ch(v.tensor_scalar_add(g3[:, 80:81], g3[:, 80:81],
                                               1e-6))
                        ch(v.reciprocal(g3[:, 80:81], g3[:, 80:81]))
                        ch(v.tensor_tensor(
                            combine_tok[:, tl * 16:(tl + 1) * 16], gw[:, :],
                            g3[:, 80:81].to_broadcast([128, 16]),
                            op=ALU.mult))

            dump("combine_tok", combine_tok[:, :], [128, 32 * NE])

        if BUILD_STAGE >= 5:
            with contextlib.ExitStack() as p3:
                scope_barrier()
                h_all = p3.enter_context(
                    sbt("h_all", [128, 72 * 512], BF16))
                xn_tb = [p3.enter_context(
                    sbt(f"xn_tb{i}", [128, 8 * 512], BF16))
                    for i in range(2)]
                at_tb = [p3.enter_context(
                    sbt(f"at_tb{i}", [128, 4 * 1024], BF16))
                    for i in range(2)]
                x_tb = [p3.enter_context(
                    sbt(f"x_tb{i}", [128, 1024], F32))
                    for i in range(4)]
                w13 = [p3.enter_context(
                    sbt(f"w13_{i}", [128, 8 * 512], BF16))
                    for i in range(4)]
                w2b = [p3.enter_context(
                    sbt(f"w2b_{i}", [128, 4 * 1024], BF16))
                    for i in range(2)]
                cbc = p3.enter_context(sbt("cbc", [128, 512], F32))
                sg = p3.enter_context(sbt("sg", [128, 2 * 512], BF16))
                us = p3.enter_context(sbt("us", [128, 2 * 512], BF16))
                ct4 = p3.enter_context(
                    sbt("ct4", [128, 4 * 512], BF16))
                out_sb = [p3.enter_context(
                    sbt(f"out_sb{i}", [128, 1024], F32))
                    for i in range(2)]

                spx_need = SC["spx"].n

                for tb in range(8):
                    tok0 = tb * 512

                    ld_before = SC["mm"].n
                    ev2_before = SC["ev2"].n
                    lsem = SC["ldxA"] if tb % 2 == 0 else SC["ldxB"]

                    @block.sync
                    def _(s):
                        s.wait_ge(SC["spx"].sem, spx_need)
                        if tb >= 1:
                            s.wait_ge(SC["ev2"].sem, ev2_before)
                        if tb >= 2:
                            s.wait_ge(SC["mm"].sem, ld_before)
                        s.dma_start(out=xn_tb[tb % 2]
                                    .rearrange("p (c t) -> p c t", t=512),
                                    in_=xnspill[:, tok0:tok0 + 512]
                                    .rearrange("(c p) t -> p c t", p=128)
                                    ).then_inc(lsem.sem, 16)
                        lsem.n += 16
                        s.dma_start(out=at_tb[tb % 2]
                                    .rearrange("p (u d) -> p u d", d=1024),
                                    in_=attnspill[tok0:tok0 + 512, :]
                                    .rearrange("(u p) d -> p u d", p=128)
                                    ).then_inc(lsem.sem, 16)
                        lsem.n += 16
                        for u in range(4):
                            s.dma_start(out=x_tb[u][:, :],
                                        in_=x_in[tok0 + u * 128:
                                                 tok0 + (u + 1) * 128, :]
                                        ).then_inc(lsem.sem, 16)
                            lsem.n += 16
                    xn_ready = lsem.n
                    xn_sem = lsem

                    @block.tensor
                    def _(t):
                        if BUILD_STAGE >= 4:
                            t.wait_ge(SC["vx"].sem, SC["vx"].n)
                        tbar(t)
                        for u in range(4):
                            gt = tb * 4 + u
                            for b_ in range(4):
                                SC["tp"].inc(t.transpose(
                                    bank(4 + b_)[0:4, u * 128:(u + 1) * 128],
                                    combine_tok[:, gt * 16 + b_:
                                                gt * 16 + 16:4],
                                    identf[:]))
                    tp_c = SC["tp"].n

                    @block.vector
                    def _(v):
                        v.wait_ge(SC["tp"].sem, tp_c)
                        for b_ in range(4):
                            SC["ev2"].inc(v.tensor_copy(
                                ct4[0:4, b_ * 512:(b_ + 1) * 512],
                                bank(4 + b_)[0:4, 0:512]))
                    ct4_ready = SC["ev2"].n

                    pa_mm_marks = {}
                    pa_sc_marks = {}
                    pa_vx_marks = {}
                    for e in range(18):
                        routed = e < NE
                        wslot = e % 2
                        wsem = SC["ldfA"] if wslot == 0 else SC["ldfB"]

                        @block.sync
                        def _(s):
                            if e >= 2:
                                s.wait_ge(SC["mm"].sem, pa_mm_marks[e - 2])
                            w1n = "w1t" if routed else "w1sts"
                            w3n = "w3t" if routed else "w3sts"
                            r0 = e * D if routed else (e - NE) * D
                            load_weight_rows(s, w1n, r0, D,
                                             w13[2 * wslot], wsem,
                                             war_wait=False)
                            load_weight_rows(s, w3n, r0, D,
                                             w13[2 * wslot + 1], wsem,
                                             war_wait=False)
                        w_ready = wsem.n

                        if routed:
                            a_, b_ = e // 4, e % 4
                            cbank = 6 + e % 2

                            @block.tensor
                            def _(t):
                                t.wait_ge(SC["ev2"].sem, ct4_ready)
                                if e == 0:
                                    tbar(t)
                                if e >= 2:
                                    t.wait_ge(SC["vx"].sem,
                                              pa_vx_marks.get(
                                                  ("cbc", e - 2), 0))
                                SC["mm2"].inc(t.matmul(
                                    bank(cbank),
                                    selb[0:4, a_ * 128:(a_ + 1) * 128],
                                    ct4[0:4, b_ * 512:(b_ + 1) * 512],
                                    start=True, stop=True))
                            cbc_mm = SC["mm2"].n

                            @block.vector
                            def _(v):
                                v.wait_ge(SC["mm2"].sem, cbc_mm)
                                SC["vx"].inc(v.tensor_copy(cbc[:, :],
                                                           bank(cbank)))
                            pa_vx_marks[("cbc", e)] = SC["vx"].n
                        cbc_ready = SC["vx"].n

                        mm_base = SC["mm"].n
                        sc_base = SC["sc"].n
                        vx_base = SC["vx"].n

                        @block.tensor
                        def _(t):
                            t.wait_ge(wsem.sem, w_ready)
                            t.wait_ge(xn_sem.sem, xn_ready)
                            if e == 0:
                                tbar(t)
                            for fc in range(4):
                                gidx = e * 4 + fc
                                if gidx >= 3:
                                    gprev = gidx - 3
                                    if gprev >= e * 4:
                                        scm = sc_base + (gprev - e * 4) + 1
                                        vxm = vx_base + 2 * (gprev - e * 4) + 2
                                    else:
                                        scm = pa_sc_marks[gprev]
                                        vxm = pa_vx_marks[gprev]
                                    t.wait_ge(SC["sc"].sem, scm)
                                    t.wait_ge(SC["vx"].sem, vxm)
                                gb = bank(gidx % 3)
                                ub = bank(3 + gidx % 3)
                                for k in range(8):
                                    t.matmul(gb,
                                             w13[2 * wslot][
                                                 :, k * 512 + fc * 128:
                                                 k * 512 + (fc + 1) * 128],
                                             xn_tb[tb % 2][
                                                 :, k * 512:(k + 1) * 512],
                                             start=(k == 0), stop=(k == 7))
                                ins = None
                                for k in range(8):
                                    ins = t.matmul(
                                        ub,
                                        w13[2 * wslot + 1][
                                            :, k * 512 + fc * 128:
                                            k * 512 + (fc + 1) * 128],
                                        xn_tb[tb % 2][:, k * 512:
                                                      (k + 1) * 512],
                                        start=(k == 0), stop=(k == 7))
                                SC["mm"].inc(ins, 2)
                        pa_mm_marks[e] = SC["mm"].n

                        @block.scalar
                        def _(s):
                            for fc in range(4):
                                gidx = e * 4 + fc
                                s.wait_ge(SC["mm"].sem, mm_base + 2 * fc + 2)
                                if gidx >= 2:
                                    gprev = gidx - 2
                                    if gprev >= e * 4:
                                        vxm = vx_base + 2 * (gprev - e * 4) + 2
                                    else:
                                        vxm = pa_vx_marks[gprev]
                                    s.wait_ge(SC["vx"].sem, vxm)
                                SC["sc"].inc(s.activation(
                                    sg[:, (fc % 2) * 512:(fc % 2 + 1) * 512],
                                    bank(gidx % 3), AF.Silu))
                                pa_sc_marks[gidx] = SC["sc"].n

                        @block.vector
                        def _(v):
                            if routed:
                                v.wait_ge(SC["vx"].sem, cbc_ready)
                            for fc in range(4):
                                gidx = e * 4 + fc
                                v.wait_ge(SC["mm"].sem, mm_base + 2 * fc + 2)
                                if routed:
                                    SC["vx"].inc(v.scalar_tensor_tensor(
                                        us[:, (fc % 2) * 512:
                                           (fc % 2 + 1) * 512],
                                        bank(3 + gidx % 3), 0.0, cbc[:, :],
                                        op0=ALU.bypass, op1=ALU.mult))
                                else:
                                    SC["vx"].inc(v.tensor_copy(
                                        us[:, (fc % 2) * 512:
                                           (fc % 2 + 1) * 512],
                                        bank(3 + gidx % 3)))
                                v.wait_ge(SC["vx"].sem, SC["vx"].n)
                                v.wait_ge(SC["sc"].sem, sc_base + fc + 1)
                                SC["vx"].inc(v.tensor_tensor(
                                    h_all[:, (4 * e + fc) * 512:
                                          (4 * e + fc + 1) * 512],
                                    sg[:, (fc % 2) * 512:(fc % 2 + 1) * 512],
                                    us[:, (fc % 2) * 512:(fc % 2 + 1) * 512],
                                    op=ALU.mult))
                                pa_vx_marks[gidx] = SC["vx"].n
                    h_done = SC["vx"].n

                    started = [[False] * 2 for _ in range(4)]

                    def w2_mms(t, wsb, fk, col0):
                        ins = None
                        for u in range(4):
                            for hf in range(2):
                                st = not started[u][hf]
                                started[u][hf] = True
                                ins = t.matmul(
                                    bank(2 * u + hf),
                                    h_all[:, (col0 + fk) * 512 + u * 128:
                                          (col0 + fk) * 512 + (u + 1) * 128],
                                    wsb[:, fk * 1024 + hf * 512:
                                        fk * 1024 + (hf + 1) * 512],
                                    start=st, stop=False)
                        return ins

                    pb_mm_marks = {}
                    for e in range(18):
                        w2sem = SC["ldf2A"] if e % 2 == 0 else SC["ldf2B"]

                        @block.sync
                        def _(s):
                            if e >= 2:
                                s.wait_ge(SC["mm"].sem, pb_mm_marks[e - 2])
                            wn = "w2t" if e < NE else "w2st"
                            r0 = e * FE if e < NE else (e - NE) * FE
                            load_weight_rows(s, wn, r0, FE, w2b[e % 2], w2sem,
                                             war_wait=False)
                        w2_ready = w2sem.n

                        @block.tensor
                        def _(t):
                            t.wait_ge(w2sem.sem, w2_ready)
                            if e == 0:
                                t.wait_ge(SC["vx"].sem, h_done)
                                tbar(t)
                            for fk in range(4):
                                ins = w2_mms(t, w2b[e % 2], fk, 4 * e)
                            SC["mm"].inc(ins)
                        pb_mm_marks[e] = SC["mm"].n

                    mm_y_marks = []

                    @block.tensor
                    def _(t):
                        for u in range(4):
                            for hf in range(2):
                                t.matmul(bank(2 * u + hf),
                                         onesb[0:1, 0:128],
                                         wob_row[0:1, hf * 512:(hf + 1) * 512],
                                         start=False, stop=False,
                                         tile_position=(0, 0))
                                ins = t.matmul(
                                    bank(2 * u + hf), identb[:],
                                    at_tb[tb % 2][:, u * 1024 + hf * 512:
                                                  u * 1024 + (hf + 1) * 512],
                                    start=False, stop=True)
                                SC["mm"].inc(ins)
                            mm_y_marks.append(SC["mm"].n)

                    out_marks = []

                    @block.vector
                    def _(v):
                        v.wait_ge(xn_sem.sem, xn_ready)
                        if SC["outA"].n:
                            v.wait_ge(SC["outA"].sem, SC["outA"].n)
                        if SC["outB"].n:
                            v.wait_ge(SC["outB"].sem, SC["outB"].n)
                        for u in range(4):
                            v.wait_ge(SC["mm"].sem, mm_y_marks[u])
                            if u >= 2:
                                v.wait_ge(SC["outA" if u % 2 == 0 else "outB"].sem,
                                          out_marks[u - 2])
                            for hf in range(2):
                                SC["ev2"].inc(v.tensor_tensor(
                                    out_sb[u % 2][:, hf * 512:(hf + 1) * 512],
                                    bank(2 * u + hf),
                                    x_tb[u][:, hf * 512:(hf + 1) * 512],
                                    op=ALU.add))
                            evn = SC["ev2"].n

                            osem = SC["outA" if u % 2 == 0 else "outB"]

                            @block.gpsimd
                            def _(g):
                                g.wait_ge(SC["ev2"].sem, evn)
                                g.dma_start(
                                    out=y_out[tok0 + u * 128:
                                              tok0 + (u + 1) * 128, :],
                                    in_=out_sb[u % 2][:, :]
                                ).then_inc(osem.sem, 16)
                                osem.n += 16
                            out_marks.append(osem.n)

            outA_all, outB_all = SC["outA"].n, SC["outB"].n

            @block.scalar
            def _(s):
                s.wait_ge(SC["outA"].sem, outA_all)
                s.wait_ge(SC["outB"].sem, outB_all)

        @block.gpsimd
        def _(g):
            g.wait_ge(SC["spx"].sem, SC["spx"].n)
            for nm_ in ("out", "outA", "outB"):
                if SC[nm_].n:
                    g.wait_ge(SC[nm_].sem, SC[nm_].n)

    return nc, dbg_outs


_PERM = np.concatenate([np.arange(0, D, 2), np.arange(1, D, 2)])


def _sel_const():
    sel = np.zeros((128, 512), dtype=np.float32)
    for a in range(4):
        sel[a, a * 128:(a + 1) * 128] = 1.0
    return sel


def make_maps(inputs):
    g = lambda k: np.asarray(inputs[k], dtype=np.float32)
    x = g("L0_attn_x")
    perm = _PERM
    common = {
        "wdq": np.ascontiguousarray(g("L0_attn_Wd_q_w")),
        "wdkv": np.ascontiguousarray(g("L0_attn_Wd_kv_w")),
        "wuq": np.ascontiguousarray(g("L0_attn_Wu_q_w")[:, perm]),
        "wqr": np.ascontiguousarray(g("L0_attn_Wq_r_w")[:, perm]),
        "wkr": np.ascontiguousarray(g("L0_attn_Wk_r_w")[:, perm]),
        "wuk": np.ascontiguousarray(g("L0_attn_Wu_k_w")[:, perm]),
        "wuv": np.ascontiguousarray(g("L0_attn_Wu_v_w")),
        "wo": np.ascontiguousarray(g("L0_attn_Wo_w")),
        "w1t": np.ascontiguousarray(
            g("L0_ffn_w1_e").transpose(0, 2, 1).reshape(NE * D, FE)),
        "w3t": np.ascontiguousarray(
            g("L0_ffn_w3_e").transpose(0, 2, 1).reshape(NE * D, FE)),
        "w2t": np.ascontiguousarray(
            g("L0_ffn_w2_e").transpose(0, 2, 1).reshape(NE * FE, D)),
        "w1sts": np.ascontiguousarray(np.concatenate(
            [g("L0_ffn_w1_s").T[:, 0:512], g("L0_ffn_w1_s").T[:, 512:1024]],
            axis=0)),
        "w3sts": np.ascontiguousarray(np.concatenate(
            [g("L0_ffn_w3_s").T[:, 0:512], g("L0_ffn_w3_s").T[:, 512:1024]],
            axis=0)),
        "w2st": np.ascontiguousarray(g("L0_ffn_w2_s").T),
        "wgt": np.ascontiguousarray(g("L0_ffn_w_gate").T),
        "bdq": np.ascontiguousarray(g("L0_attn_Wd_q_b").reshape(4, 128).T),
        "bdkv": np.ascontiguousarray(g("L0_attn_Wd_kv_b").reshape(4, 128).T),
        "buqA": np.ascontiguousarray(
            g("L0_attn_Wu_q_b")[perm][0:512].reshape(4, 128).T),
        "buqB": np.ascontiguousarray(
            g("L0_attn_Wu_q_b")[perm][512:1024].reshape(4, 128).T),
        "bqrA": np.ascontiguousarray(
            g("L0_attn_Wq_r_b")[perm][0:512].reshape(4, 128).T),
        "bqrB": np.ascontiguousarray(
            g("L0_attn_Wq_r_b")[perm][512:1024].reshape(4, 128).T),
        "bkrA": np.ascontiguousarray(
            g("L0_attn_Wk_r_b")[perm][0:512].reshape(4, 128).T),
        "bkrB": np.ascontiguousarray(
            g("L0_attn_Wk_r_b")[perm][512:1024].reshape(4, 128).T),
        "bukA": np.ascontiguousarray(
            g("L0_attn_Wu_k_b")[perm][0:512].reshape(4, 128).T),
        "bukB": np.ascontiguousarray(
            g("L0_attn_Wu_k_b")[perm][512:1024].reshape(4, 128).T),
        "buv": np.ascontiguousarray(g("L0_attn_Wu_v_b")),
        "wob": np.ascontiguousarray(g("L0_attn_Wo_b")),
        "eye": np.eye(128, dtype=np.float32),
        "selc": _sel_const(),
        "pos": np.ascontiguousarray(
            np.broadcast_to(np.arange(S, dtype=np.float32), (128, S))),
    }
    for k_ in ("wdq", "wdkv", "wuq", "wqr", "wkr", "wuk", "wuv", "wo",
               "w1t", "w3t", "w2t", "w1sts", "w3sts", "w2st", "wgt"):
        w_ = common[k_]
        r_, c_ = w_.shape
        common[k_] = np.ascontiguousarray(
            w_.reshape(r_ // 128, 128, c_).transpose(1, 0, 2)
            .reshape(128, (r_ // 128) * c_))
    maps = []
    for i in range(NCORES):
        m = dict(common)
        m["x"] = np.ascontiguousarray(x[2 * i:2 * i + 2].reshape(T, D))
        maps.append(m)
    return maps


def kernel(**inputs):
    import jax
    jax.device_put(np.zeros(64, np.float32),
                   jax.devices()[0]).block_until_ready()
    nc, _ = build_nc()
    maps = make_maps(inputs)
    res = run_bass_kernel_spmd(nc, maps, core_ids=list(range(NCORES)))
    out = np.empty((B, S, D), dtype=np.float32)
    for i in range(NCORES):
        out[2 * i:2 * i + 2] = res.results[i]["y"].reshape(2, S, D)
    return out
